# version 02
# baseline (speedup 1.0000x reference)
import numpy as np
import ml_dtypes
from contextlib import ExitStack

import concourse.bass as bass
import concourse.mybir as mybir
from concourse.bass_utils import run_bass_kernel_spmd

F32 = mybir.dt.float32
BF16 = mybir.dt.bfloat16
AF = mybir.ActivationFunctionType
ALU = mybir.AluOpType
AX = mybir.AxisListType

NCORES = 8
DM = 1024
NCH = 8
NS = 16
NE = 4
LN_EPS = 1e-5

VP = {}
_c = 0
for _nm, _n in [("ln1_w", 8), ("ln1_b", 8), ("lns_w", 8), ("lns_b", 8),
                ("ln2_w", 8), ("ln2_b", 8), ("conv_b", 8), ("out_b", 8),
                ("mlp_b2", 8), ("in_b", 16), ("mlp_b1", 16),
                ("xp_b", 1), ("A_log", 1), ("bg", 1), ("c", 8),
                ("b1", 128), ("b2", 32)]:
    VP[_nm] = _c
    _c += _n
NV = _c

C_ID = 0
C_SELT = 128
C_SEL16 = 144
C_ONEROW = 272
NC32 = 400
CB_ONECOL = 0
CB_ONEROW = 1
CB_ID = 129
CB_SELROW = 257
NCB = 257 + 512


def host_pack(inputs, T):
    d = inputs
    x = np.asarray(d["L0_attn_x"])[0]
    S = x.shape[0]
    assert S == NCORES * T
    xf = np.ascontiguousarray(np.asarray(x, np.float32).T)
    xpad = np.concatenate([np.zeros((DM, 3), np.float32), xf], axis=1)

    vec = np.zeros((128, NV), np.float32)

    def put(name, arr):
        a = np.asarray(arr, np.float32).reshape(-1)
        n = a.size // 128
        vec[:, VP[name]:VP[name] + n] = a.reshape(n, 128).T

    def put_small(name, arr):
        a = np.asarray(arr, np.float32).reshape(-1)
        vec[:a.size, VP[name]] = a

    put("ln1_w", d["L0_attn_ln1_w"]); put("ln1_b", d["L0_attn_ln1_b"])
    put("lns_w", d["L0_attn_lns_w"]); put("lns_b", d["L0_attn_lns_b"])
    put("ln2_w", d["L0_attn_ln2_w"]); put("ln2_b", d["L0_attn_ln2_b"])
    put("conv_b", d["L0_attn_conv_b"]); put("out_b", d["L0_attn_out_b"])
    put("mlp_b2", d["L0_attn_mlp_b2"]); put("in_b", d["L0_attn_in_b"])
    put("mlp_b1", d["L0_attn_mlp_b1"])
    put_small("xp_b", d["L0_attn_xp_b"]); put_small("A_log", d["L0_attn_A_log"])
    put_small("bg", d["L0_ffn_bg"])
    put("c", d["L0_c"]); put("b1", d["L0_ffn_b1"]); put("b2", d["L0_ffn_b2"])

    cf32 = np.zeros((128, NC32), np.float32)
    cf32[:, C_ID:C_ID + 128] = np.eye(128)
    for p in range(128):
        cf32[p, C_SELT + (p % 16)] = 1.0
    for k in range(128):
        cf32[k % 16, C_SEL16 + k] = 1.0
    cf32[0, C_ONEROW:C_ONEROW + 128] = 1.0

    cbf = np.zeros((128, NCB), ml_dtypes.bfloat16)
    cbf[:, CB_ONECOL] = 1.0
    cbf[0, CB_ONEROW:CB_ONEROW + 128] = 1.0
    cbf[:, CB_ID:CB_ID + 128] = np.eye(128, dtype=ml_dtypes.bfloat16)
    for e in range(4):
        cbf[e, CB_SELROW + e * 128:CB_SELROW + (e + 1) * 128] = 1.0

    alog_row = np.asarray(d["L0_attn_A_log"], np.float32).reshape(1, NS)
    conv_w = np.asarray(d["L0_attn_conv_w"], np.float32).reshape(DM, 4)
    convw = np.zeros((128, 32), np.float32)
    for j in range(4):
        for kc in range(NCH):
            convw[:, j * 8 + kc] = conv_w[kc * 128:(kc + 1) * 128, j]

    in_maps = []
    for k in range(NCORES):
        cexp = np.zeros((128, T), np.float64)
        T0 = k * T
        t = np.arange(T, dtype=np.float64)
        for j in range(NCORES):
            if j < k:
                cexp[j * 16:(j + 1) * 16, :] = ((T0 - 1.0) - (j * T + t))[None, :]
            else:
                cexp[j * 16:(j + 1) * 16, :] = 1e9
        m = {
            "x": np.ascontiguousarray(xpad[:, k * T:k * T + T + 3]),
            "hmask": np.full((128, 1), 0.0 if k == 0 else 1.0, np.float32),
            "vecpack": vec, "cf32": cf32, "cbf": cbf,
            "alog_row": alog_row, "convw": convw,
            "cexp": cexp.astype(ml_dtypes.bfloat16),
            "in_w": np.asarray(d["L0_attn_in_w"], np.float32),
            "xp_w": np.asarray(d["L0_attn_xp_w"], np.float32),
            "d_mat": np.asarray(d["L0_attn_D"], np.float32),
            "out_w": np.asarray(d["L0_attn_out_w"], np.float32),
            "mlp_w1": np.asarray(d["L0_attn_mlp_w1"], np.float32),
            "mlp_w2": np.asarray(d["L0_attn_mlp_w2"], np.float32),
            "ffn_w1": np.asarray(d["L0_ffn_W1"], np.float32),
            "ffn_w2": np.asarray(d["L0_ffn_W2"], np.float32),
            "wg": np.asarray(d["L0_ffn_Wg"], np.float32),
            "b2m": np.asarray(d["L0_ffn_b2"], np.float32),
            "wmod": np.asarray(d["L0_wmod"], np.float32),
        }
        in_maps.append(m)
    return in_maps


class Sem:
    def __init__(self, stk, nc, name):
        self.h = stk.enter_context(nc.semaphore(name))
        self.n = 0


class PsumT:
    def __init__(self, handle):
        self.t = handle
        self.free = None


def build_program(T, stop_after=None):
    TH = T + 3
    SCH = T // 128
    Th = T // 2
    NT = max(1, T // 512)
    TT = min(512, T)
    NTC = T // 128
    NHB = max(1, Th // 512)
    STP = min(512, Th)
    THS = []
    c0 = 0
    while c0 < TH:
        w = min(512, TH - c0)
        THS.append((c0, w))
        c0 += w
    TS = [(3 + s * TT, min(TT, T - s * TT)) for s in range(NT)]

    W_TSCR = max(TH, 2051)
    O_XBUF = 0
    O_TSCR = O_XBUF + NCH * TH
    O_ROWS = O_TSCR + W_TSCR
    O_PPOW = O_ROWS
    O_HFM = O_ROWS + T
    O_GATE = O_ROWS
    O_CST = O_ROWS + 2 * TH
    O_VEC = O_CST
    O_CF32 = O_VEC + NV
    O_CW = O_CF32 + NC32
    O_MCOL = O_CW + 32
    O_ACOL = O_MCOL + 48
    O_BCOL = O_ACOL + 8
    O_GACOL = O_BCOL + 8
    O_GBCOL = O_GACOL + 8
    O_CLAMP = O_GBCOL + 8
    O_GMX = O_CLAMP + 8
    O_GTM = O_GMX + 8
    O_HM = O_GTM + 4
    O_SELTM = O_HM + 1
    O_HLOC = O_SELTM + 4 * NTC
    O_EVEC = O_HLOC + 16 * SCH
    O_CARRY = O_EVEC + (SCH + 1)
    O_A128 = O_CARRY + 1
    O_LAMC = O_A128 + 1
    O_LAMT = O_LAMC + 1
    O_RSUM = O_LAMT + 1
    O_LAMN = O_RSUM + 1
    O_LROW = O_LAMN + 16
    O_ALR = O_LROW + 16
    NF32 = O_ALR + 16

    B_ACT = 0
    B_WA = B_ACT + NCH * TH
    B_WB = B_WA + 8192
    B_X = B_WB + 8192
    B_SA = B_X
    B_SB = B_X + 4096
    B_SC = B_X + 8192
    B_SD = B_X + 12288
    ZSLAB = 8 * TT
    RBW = max(2 * TH, ZSLAB)
    B_RB = B_X + 16384
    B_MRB = B_RB + TH
    B_DREG = B_RB + RBW
    B_CBF = B_DREG + 1024
    B_XPW = B_CBF + NCB
    B_WGB = B_XPW + 128
    B_TALL = B_WGB + 32
    B_RMR = B_TALL + 16 * SCH
    NBF = B_RMR + 32

    B_XPT = B_SC
    B_ZW0, B_ZW1 = B_SA, B_SB
    B_OUTW = B_SA
    B_TTALL = B_SC
    B_HBF = B_SC + 2048
    B_YSZ = B_SD
    B_CEXP = B_SD
    B_UALL = B_SD + 2048
    B_G1 = B_SA
    B_XSQ = B_SD
    B_HMI = B_SC
    B_UFM = B_RB
    B_ZRD = B_RB
    B_MASK4 = B_RB
    B_MBC = B_RB + T

    global LAYOUT
    LAYOUT = {k: v for k, v in locals().items() if k.startswith(("O_", "B_", "N"))}
    nc = bass.Bass("TRN2", target_bir_lowering=False, detect_race_conditions=False)

    x_h = nc.dram_tensor("x", [DM, TH], F32, kind="ExternalInput").ap()
    vp_h = nc.dram_tensor("vecpack", [128, NV], F32, kind="ExternalInput").ap()
    cf_h = nc.dram_tensor("cf32", [128, NC32], F32, kind="ExternalInput").ap()
    cb_h = nc.dram_tensor("cbf", [128, NCB], BF16, kind="ExternalInput").ap()
    al_h = nc.dram_tensor("alog_row", [1, NS], F32, kind="ExternalInput").ap()
    hm_h = nc.dram_tensor("hmask", [128, 1], F32, kind="ExternalInput").ap()
    cw_h = nc.dram_tensor("convw", [128, 32], F32, kind="ExternalInput").ap()
    ce_h = nc.dram_tensor("cexp", [128, T], BF16, kind="ExternalInput").ap()
    inw_h = nc.dram_tensor("in_w", [DM, 2048], F32, kind="ExternalInput").ap()
    xpw_h = nc.dram_tensor("xp_w", [DM, NS], F32, kind="ExternalInput").ap()
    dm_h = nc.dram_tensor("d_mat", [NS, DM], F32, kind="ExternalInput").ap()
    ow_h = nc.dram_tensor("out_w", [DM, DM], F32, kind="ExternalInput").ap()
    m1_h = nc.dram_tensor("mlp_w1", [DM, 2048], F32, kind="ExternalInput").ap()
    m2_h = nc.dram_tensor("mlp_w2", [2048, DM], F32, kind="ExternalInput").ap()
    f1_h = nc.dram_tensor("ffn_w1", [NE, DM, 4096], F32, kind="ExternalInput").ap()
    f2_h = nc.dram_tensor("ffn_w2", [NE, 4096, DM], F32, kind="ExternalInput").ap()
    wg_h = nc.dram_tensor("wg", [DM, NE], F32, kind="ExternalInput").ap()
    b2m_h = nc.dram_tensor("b2m", [NE, DM], F32, kind="ExternalInput").ap()
    wm_h = nc.dram_tensor("wmod", [DM, 6 * DM], F32, kind="ExternalInput").ap()
    out_h = nc.dram_tensor("out", [DM, T], F32, kind="ExternalOutput").ap()
    ccin = nc.dram_tensor("ccin", [NS, T], BF16, kind="Internal")
    ccout = nc.dram_tensor("ccout", [128, T], BF16, kind="Internal")
    zd = nc.dram_tensor("zd", [128, NCH * T], BF16, kind="Internal")

    stk = ExitStack()
    with stk:
        AF32 = stk.enter_context(nc.sbuf_tensor("af32", [128, NF32], F32))
        ABF = stk.enter_context(nc.sbuf_tensor("abf", [128, NBF], BF16))

        def fa(p0, pn, o, w):
            return AF32[p0:p0 + pn, o:o + w]

        def ba(p0, pn, o, w):
            return ABF[p0:p0 + pn, o:o + w]

        PA = PsumT(stk.enter_context(nc.psum_tensor("PA", [128, 1024], F32)))
        PB = PsumT(stk.enter_context(nc.psum_tensor("PB", [128, 1024], F32)))
        PC = PsumT(stk.enter_context(nc.psum_tensor("PC", [128, 1024], F32)))
        PD = PsumT(stk.enter_context(nc.psum_tensor("PD", [128, 1024], F32)))

        SM = Sem(stk, nc, "sm")
        SMA = Sem(stk, nc, "sma")
        SMB = Sem(stk, nc, "smb")
        SX = Sem(stk, nc, "sx")
        SXA = Sem(stk, nc, "sxa")
        SXB = Sem(stk, nc, "sxb")
        SZE = Sem(stk, nc, "sze")
        SZO = Sem(stk, nc, "szo")
        SZR = Sem(stk, nc, "szr")
        SO = Sem(stk, nc, "so")
        G1 = Sem(stk, nc, "g1")
        GF = Sem(stk, nc, "gf")
        GW0 = Sem(stk, nc, "gw0")
        GW1 = Sem(stk, nc, "gw1")
        GU = Sem(stk, nc, "gu")
        CC = Sem(stk, nc, "cc")
        GP = Sem(stk, nc, "gp")
        TN = Sem(stk, nc, "tn")
        VE = Sem(stk, nc, "ve")
        SC = Sem(stk, nc, "sc")
        GWs = [GW0, GW1]
        WBo = [B_WA, B_WB]
        _ENG_SEM_INIT = True

        Q = {k: [] for k in ("sync", "gpsimd", "tensor", "vector", "scalar")}
        _bstate = {"stopped": False}

        ENG_SEM = {}

        def op(eng, fn, waits=(), incs=(), chain=False):
            if _bstate["stopped"]:
                return
            waits = [w for w in waits if w is not None]
            if chain:
                s = ENG_SEM[eng]
                waits.append((s, s.n))
                incs = list(incs) + [(s, 1)]
            merged = {}
            for s, k in incs:
                merged[id(s)] = (s, merged.get(id(s), (s, 0))[1] + k)
            incs = list(merged.values())
            rincs = []
            for s, k in incs:
                s.n += k
                rincs.append((s, k))

            def run(e, _fn=fn, _w=waits, _i=rincs):
                for s, v in _w:
                    e.wait_ge(s.h, v)
                inst = _fn(e)
                for s, k in _i:
                    if inst is None:
                        inst = e.nop()
                    inst = inst.then_inc(s.h, k)
            Q[eng].append(run)

        ENG_SEM.update({"vector": VE, "scalar": SC, "tensor": TN,
                        "gpsimd": GP})

        def wv(sem):
            return (sem, sem.n)

        def dma(eng, sem, out, in_, waits=(), slow=False):
            op(eng, lambda e, o=out, i=in_, s=slow: e.dma_start(
                out=o, in_=i, allow_slow_non_contiguous=s),
               waits=waits, incs=[(sem, 16)])

        def mm(p, out_sl, lhsT, rhs, start, stop, waits=(), inc=False):
            op("tensor", lambda e, _o=out_sl, _l=lhsT, _r=rhs, _s=start, _p=stop:
               e.matmul(_o, _l, _r, start=_s, stop=_p),
               waits=waits, incs=([(TN, 1)] if inc else []))

        def evac(eng_name, p, fn, waits=(), last=True):
            sem = VE if eng_name == "vector" else SC
            op(eng_name, fn, waits=waits, incs=[(sem, 1)])
            if last and p is not None:
                p.free = wv(sem)
            return wv(sem)

        def ckpt(name, tok, src_off=None):
            if stop_after != name or _bstate["stopped"]:
                return
            so = O_XBUF if src_off is None else src_off
            for oc in range(NCH):
                dma("sync", SO, out_h[oc * 128:(oc + 1) * 128, 0:T],
                    fa(0, 128, so + oc * TH + 3, T),
                    waits=([tok] if oc == 0 else []))
            op("gpsimd", lambda e: e.wait_ge(SO.h, SO.n))
            _bstate["stopped"] = True

        onecol = ba(0, 128, B_CBF + CB_ONECOL, 1)
        onerow_b = ba(0, 1, B_CBF + CB_ONEROW, 128)
        onerow_f = fa(0, 1, O_CF32 + C_ONEROW, 128)
        id128f = fa(0, 128, O_CF32 + C_ID, 128)
        id16b = ba(0, NS, B_CBF + CB_ID, NS)
        id4f = fa(0, 4, O_CF32 + C_ID, 4)
        selt = fa(0, 128, O_CF32 + C_SELT, NS)
        sel16 = fa(0, NS, O_CF32 + C_SEL16, 128)

        def vcol(name, i=0):
            return fa(0, 128, O_VEC + VP[name] + i, 1)

        def act_ch(kc, off, w):
            return ba(0, 128, B_ACT + kc * TH + off, w)

        def xbuf_ch(kc, off, w):
            return fa(0, 128, O_XBUF + kc * TH + off, w)

        dma("sync", SM, fa(0, 128, O_VEC, NV), vp_h)
        dma("sync", SM, fa(0, 128, O_CF32, NC32), cf_h)
        dma("sync", SM, ba(0, 128, B_CBF, NCB), cb_h)
        dma("sync", SM, fa(0, 1, O_ALR, NS), al_h)
        dma("sync", SM, fa(0, 128, O_CW, 32), cw_h)
        dma("sync", SM, fa(0, 128, O_HM, 1), hm_h)
        sm_all = wv(SM)

        for c in range(NCH):
            dma("gpsimd", G1, act_ch(c, 0, TH), x_h[c * 128:(c + 1) * 128, :])
        g1_all = wv(G1)

        def wmod_src(sec):
            return wm_h[:, sec * DM:(sec + 1) * DM].rearrange(
                "(a p) c -> p a c", p=128)

        def wbuf_view(i):
            return ba(0, 128, WBo[i], 8192).rearrange("p (a c) -> p a c", a=8)

        dma("gpsimd", GF, ba(0, 128, B_XPW, 128).rearrange("p (a c) -> p a c", a=8),
            xpw_h.rearrange("(a p) c -> p a c", p=128))
        dma("gpsimd", GF, ba(0, NS, B_DREG, DM), dm_h)
        dma("gpsimd", GF, ba(0, 128, B_WGB, 32).rearrange("p (a c) -> p a c", a=8),
            wg_h.rearrange("(a p) c -> p a c", p=128))
        gf_small = wv(GF)

        op("scalar", lambda e: e.activation(fa(0, 128, O_CLAMP, 8),
                                            fa(0, 128, O_VEC + VP["c"], 8),
                                            AF.Silu),
           waits=[sm_all], incs=[(SC, 1)])
        sc_silc = wv(SC)
        prev_chunk_tn = {}
        for sec in range(6):
            for kc in range(8):
                gi = sec * 8 + kc
                slot = O_XBUF + (gi % 2) * 1024
                p = PC if gi % 2 == 0 else PD
                psem = SMA if gi % 2 == 0 else SMB
                lw = [prev_chunk_tn[gi - 2]] if gi >= 2 else []
                dma("sync", psem, fa(0, 128, slot, 1024),
                    wm_h[kc * 128:(kc + 1) * 128, sec * DM:(sec + 1) * DM],
                    waits=lw)
                sm_c = wv(psem)
                wts0 = [sm_c, sc_silc, p.free]
                for oc in range(8):
                    mm(p, p.t[:, oc:oc + 1],
                       fa(0, 128, slot + oc * 128, 128),
                       fa(0, 128, O_CLAMP + kc, 1),
                       True, True,
                       waits=(wts0 if oc == 0 else ()),
                       inc=(oc == 7))
                prev_chunk_tn[gi] = wv(TN)
                ve_pr = wv(VE)
                if kc == 0:
                    evac("vector", p, lambda e, _s=sec, _p=p:
                         e.tensor_copy(fa(0, 128, O_MCOL + _s * 8, 8),
                                       _p.t[:, 0:8]),
                         waits=[prev_chunk_tn[gi], ve_pr])
                else:
                    evac("vector", p, lambda e, _s=sec, _p=p:
                         e.tensor_add(fa(0, 128, O_MCOL + _s * 8, 8),
                                      fa(0, 128, O_MCOL + _s * 8, 8),
                                      _p.t[:, 0:8]),
                         waits=[prev_chunk_tn[gi], ve_pr])
        sc_modall = wv(VE)
        tn_modall = wv(TN)
        dma("gpsimd", GW0, wbuf_view(0),
            inw_h[:, 0:1024].rearrange("(a p) c -> p a c", p=128))
        inw0_tok = wv(GW0)
        dma("gpsimd", GW1, wbuf_view(1),
            inw_h[:, 1024:2048].rearrange("(a p) c -> p a c", p=128))
        inw1_tok = wv(GW1)

        op("vector", lambda e: e.tensor_scalar(fa(0, 128, O_ACOL, 8),
                                               fa(0, 128, O_MCOL + 8, 8), 1.0,
                                               None, ALU.add),
           waits=[sc_modall], chain=True, incs=[(VE, 1)])
        op("vector", lambda e: e.tensor_scalar(fa(0, 128, O_BCOL, 8),
                                               fa(0, 128, O_MCOL + 32, 8), 1.0,
                                               None, ALU.add),
           chain=True, incs=[(VE, 1)])

        def emit_clamp(dsto, sec):
            op("vector", lambda e: e.memset(fa(0, 128, O_GMX, 8), 0.0),
               waits=[sc_modall], chain=True)
            op("vector", lambda e, _s=sec: e.tensor_tensor(
                fa(0, 128, O_GMX, 8), fa(0, 128, O_MCOL + _s * 8, 8),
                fa(0, 128, O_GMX, 8), ALU.is_ge), chain=True)
            op("vector", lambda e: e.tensor_scalar(
                fa(0, 128, O_GMX, 8), fa(0, 128, O_GMX, 8), 2.0, -1.0,
                ALU.mult, ALU.add), chain=True, incs=[(VE, 1)])
            op("scalar", lambda e, _s=sec: e.activation(
                fa(0, 128, O_CLAMP, 8), fa(0, 128, O_MCOL + _s * 8, 8), AF.Abs),
               waits=[wv(VE), sc_modall], chain=True, incs=[(SC, 1)])
            op("vector", lambda e: e.tensor_scalar(
                fa(0, 128, O_CLAMP, 8), fa(0, 128, O_CLAMP, 8), 1.0, -0.25,
                ALU.mult, ALU.add), waits=[wv(SC)], chain=True)
            op("vector", lambda e: e.tensor_relu(
                fa(0, 128, O_CLAMP, 8), fa(0, 128, O_CLAMP, 8)), chain=True)
            op("vector", lambda e: e.tensor_scalar(
                fa(0, 128, O_CLAMP, 8), fa(0, 128, O_CLAMP, 8), 1.0, 0.25,
                ALU.mult, ALU.add), chain=True)
            op("vector", lambda e, _d=dsto: e.tensor_mul(
                fa(0, 128, _d, 8), fa(0, 128, O_GMX, 8),
                fa(0, 128, O_CLAMP, 8)), chain=True, incs=[(VE, 1)])

        emit_clamp(O_GACOL, 2)
        emit_clamp(O_GBCOL, 5)
        ve_cols = wv(VE)

        rows_s1 = lambda c0, w: fa(0, 1, O_ROWS + c0, w)
        rows_s2 = lambda c0, w: fa(0, 1, O_ROWS + TH + c0, w)
        trow = lambda c0, w: fa(0, 1, O_TSCR + c0, w)
        RBs = lambda c0, w: ba(0, 128, B_RB + c0, w)
        MRBs = lambda c0, w: ba(0, 128, B_MRB + c0, w)

        def emit_stats(use_slices, in_ready):
            tn_prev = None
            for si, (sc0, wd) in enumerate(use_slices):
                sq_waits = [in_ready] if si == 0 else [tn_prev]
                for kc in range(NCH):
                    op("scalar", lambda e, _kc=kc, _c0=sc0, _wd=wd:
                       e.activation(ba(0, 128, B_XSQ + _kc * 512, _wd),
                                    act_ch(_kc, _c0, _wd), AF.Square),
                       waits=(sq_waits if kc == 0 else ()),
                       incs=([(SC, 1)] if kc == NCH - 1 else []))
                sc_sq = wv(SC)
                wts1 = [PD.free] + ([in_ready] if si == 0 else [])
                for kc in range(NCH):
                    mm(PD, PD.t[0:1, 0:wd], onecol, act_ch(kc, sc0, wd),
                       kc == 0, kc == NCH - 1, waits=(wts1 if kc == 0 else ()))
                for kc in range(NCH):
                    mm(PD, PD.t[0:1, 512:512 + wd], onecol,
                       ba(0, 128, B_XSQ + kc * 512, wd),
                       kc == 0, kc == NCH - 1,
                       waits=([sc_sq] if kc == 0 else ()), inc=(kc == NCH - 1))
                tn_prev = wv(TN)
                evac("vector", PD, lambda e, _c0=sc0, _wd=wd:
                     e.tensor_copy(rows_s1(_c0, _wd), PD.t[0:1, 0:_wd]),
                     waits=[tn_prev], last=False)
                evac("vector", PD, lambda e, _c0=sc0, _wd=wd:
                     e.tensor_copy(rows_s2(_c0, _wd), PD.t[0:1, 512:512 + _wd]))
            W0 = use_slices[0][0]
            W1 = use_slices[-1][0] + use_slices[-1][1]
            w = W1 - W0
            op("vector", lambda e: e.tensor_scalar(rows_s1(W0, w), rows_s1(W0, w),
                                                   1.0 / DM, None, ALU.mult),
               chain=True)
            op("vector", lambda e: e.tensor_mul(trow(W0, w), rows_s1(W0, w),
                                                rows_s1(W0, w)), chain=True)
            op("vector", lambda e: e.tensor_scalar(rows_s2(W0, w), rows_s2(W0, w),
                                                   1.0 / DM, LN_EPS, ALU.mult,
                                                   ALU.add), chain=True)
            op("vector", lambda e: e.tensor_sub(rows_s2(W0, w), rows_s2(W0, w),
                                                trow(W0, w)),
               chain=True, incs=[(VE, 1)])
            ve_var = wv(VE)
            op("scalar", lambda e: e.activation(rows_s2(W0, w), rows_s2(W0, w),
                                                AF.Sqrt),
               waits=[ve_var], incs=[(SC, 1)])
            sc_sqrt = wv(SC)
            op("vector", lambda e: e.reciprocal(rows_s2(W0, w), rows_s2(W0, w)),
               waits=[sc_sqrt], chain=True)
            op("vector", lambda e: e.tensor_mul(rows_s1(W0, w), rows_s1(W0, w),
                                                rows_s2(W0, w)),
               chain=True, incs=[(VE, 1)])
            ve_rows = wv(VE)
            for bi, (dsto, ro, p) in enumerate(
                    [(B_RB, O_ROWS + TH, PB), (B_MRB, O_ROWS, PC)]):
                pend = None
                for si, (sc0, wd) in enumerate(use_slices):
                    po = (sc0 - W0) % 1024
                    fw = (pend is None)
                    if pend is None:
                        pend = sc0
                    op("tensor", lambda e, _c0=sc0, _wd=wd, _po=po, _p=p, _ro=ro:
                       e.matmul(_p.t[:, _po:_po + _wd], onerow_f,
                                fa(0, 1, _ro + _c0, _wd), start=True, stop=True),
                       waits=([_w for _w in (p.free, ve_rows) if _w] if fw else ()),
                       incs=([(TN, 1)] if po + wd >= 1024 or si == len(use_slices) - 1 else []))
                    if po + wd >= 1024 or si == len(use_slices) - 1:
                        tn_b = wv(TN)
                        lo, hi, plo = pend, sc0 + wd, (pend - W0) % 1024
                        evac("vector", p, lambda e, _lo=lo, _hi=hi, _plo=plo,
                             _d=dsto, _p=p:
                             e.tensor_copy(ba(0, 128, _d + _lo, _hi - _lo),
                                           _p.t[:, _plo:_plo + _hi - _lo]),
                             waits=[tn_b])
                        pend = None
            return wv(VE)

        def emit_affine(dst_is_xbuf, use_cols, scol, bcol_ap_fn, out_bf=True,
                        first_wait=()):
            off, w = use_cols
            toks = []
            for kc in range(NCH):
                op("vector", lambda e, _kc=kc: e.tensor_mul(
                    fa(0, 128, O_TSCR, w), act_ch(_kc, off, w), RBs(off, w)),
                   waits=(list(first_wait) if kc == 0 else ()))
                op("vector", lambda e: e.tensor_sub(
                    fa(0, 128, O_TSCR, w), fa(0, 128, O_TSCR, w), MRBs(off, w)))
                sa, ba_ = scol(kc), bcol_ap_fn(kc)
                if dst_is_xbuf:
                    dst = xbuf_ch(kc, off, w)
                else:
                    dst = act_ch(kc, off, w)
                op("vector", lambda e, _d=dst, _s=sa, _b=ba_: e.tensor_scalar(
                    _d, fa(0, 128, O_TSCR, w), _s, _b, ALU.mult, ALU.add),
                   incs=[(VE, 1)])
                toks.append(wv(VE))
            return wv(VE)

        ve_st_a = emit_stats(THS, g1_all)
        emit_affine(True, (0, TH),
                    lambda kc: fa(0, 128, O_ACOL + kc, 1),
                    lambda kc: fa(0, 128, O_MCOL + kc, 1),
                    first_wait=[ve_st_a, ve_cols, tn_modall])
        ve_h = wv(VE)
        ckpt("h", ve_h)

        for kc in range(NCH):
            op("vector", lambda e, _kc=kc: e.tensor_copy(
                act_ch(_kc, 0, TH), xbuf_ch(_kc, 0, TH)),
               incs=([(VE, 1)] if kc == NCH - 1 else []))
        ve_hb16 = wv(VE)
        ve_st_b = emit_stats(THS, ve_hb16)
        emit_affine(False, (0, TH),
                    lambda kc: vcol("ln1_w", kc), lambda kc: vcol("ln1_b", kc),
                    first_wait=[ve_st_b])
        for kc in range(NCH):
            op("vector", lambda e, _kc=kc: e.tensor_scalar(
                act_ch(_kc, 0, 3), act_ch(_kc, 0, 3), fa(0, 128, O_HM, 1),
                None, ALU.mult),
               waits=([sm_all] if kc == 0 else ()),
               incs=([(VE, 1)] if kc == NCH - 1 else []))
        ve_xn = wv(VE)

        for kc in range(NCH):
            for t0 in range(0, T, 512):
                wd = min(512, T - t0)
                op("vector", lambda e, _kc=kc, _t0=t0, _wd=wd: e.tensor_scalar(
                    fa(0, 128, O_TSCR, _wd), act_ch(_kc, _t0, _wd),
                    fa(0, 128, O_CW + _kc, 1), None, ALU.mult),
                   waits=([ve_xn, sm_all] if (kc == 0 and t0 == 0) else ()))
                for j in (1, 2, 3):
                    op("vector", lambda e, _kc=kc, _t0=t0, _wd=wd, _j=j:
                       e.scalar_tensor_tensor(
                           fa(0, 128, O_TSCR, _wd), act_ch(_kc, _t0 + _j, _wd),
                           fa(0, 128, O_CW + _j * 8 + _kc, 1),
                           fa(0, 128, O_TSCR, _wd), ALU.mult, ALU.add))
                op("vector", lambda e, _kc=kc, _t0=t0, _wd=wd:
                   e.scalar_tensor_tensor(
                       xbuf_ch(_kc, 3 + _t0, _wd), fa(0, 128, O_TSCR, _wd),
                       vcol("conv_b", _kc), xbuf_ch(_kc, 3 + _t0, _wd),
                       ALU.add, ALU.add),
                   incs=[(VE, 1)])
        ve_x2 = wv(VE)
        ckpt("x2", ve_x2)

        for kc in range(NCH):
            op("vector", lambda e, _kc=kc: e.tensor_copy(
                act_ch(_kc, 0, TH), xbuf_ch(_kc, 0, TH)),
               incs=([(VE, 1)] if kc == NCH - 1 else []))
        ve_x2b = wv(VE)
        ve_st_s = emit_stats(TS, ve_x2b)
        emit_affine(False, (3, T),
                    lambda kc: vcol("lns_w", kc), lambda kc: vcol("lns_b", kc),
                    first_wait=[ve_st_s])
        ve_xn2 = wv(VE)
        ckpt("lns", ve_xn2)

        tn_u_prev = None
        sz_writes = 0
        for ti in range(NT):
            t0 = 3 + ti * TT
            wd = min(TT, T - ti * TT)
            zslot = B_ZW0 if ti % 2 == 0 else B_ZW1
            for oc in range(16):
                p = PA if oc % 2 == 0 else PB
                wts = [p.free]
                if ti == 0 and oc == 0:
                    wts += [ve_xn2, inw0_tok]
                if ti == 0 and oc == 8:
                    wts += [inw1_tok]
                half = 0 if oc < 8 else 1
                occ = oc % 8
                for kc in range(NCH):
                    mm(p, p.t[:, 0:wd],
                       ba(0, 128, WBo[half] + kc * 1024 + occ * 128, 128),
                       act_ch(kc, t0, wd), kc == 0, kc == NCH - 1,
                       waits=(wts if kc == 0 else ()), inc=(kc == NCH - 1))
                tn_ip = wv(TN)
                if oc < 8:
                    ewts = [tn_ip] + ([tn_u_prev] if (oc == 0 and tn_u_prev) else [])
                    evac("scalar", p, lambda e, _oc=oc, _wd=wd, _p=p:
                         e.activation(ba(0, 128, B_XPT + _oc * 512, _wd),
                                      _p.t[:, 0:_wd], AF.Silu,
                                      bias=vcol("in_b", _oc)),
                         waits=ewts)
                else:
                    ewts = [tn_ip]
                    if oc == 8 and ti >= 2:
                        zsem = SZE if ti % 2 == 0 else SZO
                        ewts.append((zsem, zsem.n))
                    evac("scalar", p, lambda e, _oc=oc, _wd=wd, _p=p, _zs=zslot:
                         e.activation(ba(0, 128, _zs + (_oc - 8) * TT, _wd),
                                      _p.t[:, 0:_wd], AF.Sigmoid,
                                      bias=vcol("in_b", _oc)),
                         waits=ewts)
            sc_z = wv(SC)
            dma("sync", SZE if ti % 2 == 0 else SZO,
                zd.ap().rearrange("p (a t) -> p a t", a=8)[:, :, ti * TT:ti * TT + wd],
                ba(0, 128, zslot, ZSLAB).rearrange("p (a t) -> p a t", a=8)[:, :, 0:wd],
                waits=[sc_z])
            sz_writes += 1
            sc_xp = wv(SC)
            wts = [PC.free, sc_xp] + ([gf_small] if ti == 0 else [])
            for kc in range(NCH):
                mm(PC, PC.t[0:NS, 0:wd],
                   ba(0, 128, B_XPW + kc * 16, 16),
                   ba(0, 128, B_XPT + kc * 512, wd),
                   kc == 0, kc == NCH - 1,
                   waits=(wts if kc == 0 else ()),
                   inc=(kc == NCH - 1))
            tn_u_prev = wv(TN)
            evac("scalar", PC, lambda e, _ti=ti, _wd=wd: e.add(
                ba(0, NS, B_UFM + _ti * TT, _wd), PC.t[0:NS, 0:_wd],
                fa(0, NS, O_VEC + VP["xp_b"], 1)),
                 waits=[tn_u_prev])
        sc_u = wv(SC)
        sz_wr_all = [wv(SZE), wv(SZO)]

        op("scalar", lambda e: e.activation(fa(0, 1, O_LROW, NS),
                                            fa(0, 1, O_ALR, NS), AF.Exp),
           waits=[sm_all], incs=[(SC, 1)])
        op("vector", lambda e: e.tensor_scalar(fa(0, 1, O_LROW, NS),
                                               fa(0, 1, O_LROW, NS), -1.0,
                                               None, ALU.mult),
           waits=[wv(SC)], chain=True, incs=[(VE, 1)])
        ve_lrow = wv(VE)
        op("tensor", lambda e: e.matmul(PD.t[:, 0:NS], onerow_f,
                                        fa(0, 1, O_LROW, NS),
                                        start=True, stop=True),
           waits=[PD.free, ve_lrow], incs=[(TN, 1)])
        evac("vector", PD, lambda e: e.tensor_copy(fa(0, 128, O_LAMN, NS),
                                                   PD.t[:, 0:NS]),
             waits=[wv(TN)])
        ve_lamn = wv(VE)

        op("scalar", lambda e: e.activation(fa(0, NS, O_LAMC, 1),
                                            fa(0, NS, O_VEC + VP["A_log"], 1),
                                            AF.Exp),
           waits=[sm_all], incs=[(SC, 1)])
        sc_lamc = wv(SC)
        op("vector", lambda e: e.tensor_scalar(fa(0, NS, O_A128, 1),
                                               fa(0, NS, O_LAMC, 1), -128.0,
                                               None, ALU.mult),
           waits=[sc_lamc], chain=True, incs=[(VE, 1)])
        op("scalar", lambda e: e.activation(fa(0, NS, O_A128, 1),
                                            fa(0, NS, O_A128, 1), AF.Exp),
           waits=[wv(VE)], chain=True, incs=[(SC, 1)])
        sc_a128 = wv(SC)

        tn_now = wv(TN)
        op("gpsimd", lambda e: e.iota(fa(0, NS, O_PPOW, T), pattern=[[1, T]],
                                      base=1, channel_multiplier=0,
                                      allow_small_or_imprecise_dtypes=True),
           waits=[tn_now], incs=[(GP, 1)])
        gp_i1 = wv(GP)
        op("vector", lambda e: e.tensor_scalar(fa(0, NS, O_PPOW, T),
                                               fa(0, NS, O_PPOW, T),
                                               fa(0, NS, O_LAMC, 1), None,
                                               ALU.mult),
           waits=[gp_i1, sc_lamc, tn_now], chain=True)
        op("vector", lambda e: e.tensor_scalar(fa(0, NS, O_PPOW, T),
                                               fa(0, NS, O_PPOW, T), -1.0,
                                               None, ALU.mult),
           chain=True, incs=[(VE, 1)])
        op("scalar", lambda e: e.activation(fa(0, NS, O_PPOW, T),
                                            fa(0, NS, O_PPOW, T), AF.Exp),
           waits=[wv(VE)], chain=True, incs=[(SC, 1)])
        sc_ppow = wv(SC)

        op("gpsimd", lambda e: e.iota(fa(0, 128, O_TSCR, 128),
                                      pattern=[[1, 128]], base=0,
                                      channel_multiplier=-1,
                                      allow_small_or_imprecise_dtypes=True),
           waits=[ve_xn2], incs=[(GP, 1)])
        gp_i2 = wv(GP)
        op("gpsimd", lambda e: e.affine_select(fa(0, 128, O_TSCR, 128),
                                               fa(0, 128, O_TSCR, 128),
                                               pattern=[[1, 128]],
                                               compare_op=ALU.is_ge, fill=1e9,
                                               base=0, channel_multiplier=-1),
           waits=[gp_i2], incs=[(GP, 1)])
        gp_dm = wv(GP)
        sc_tt_prev = {0: None, 1: None}
        for n in range(NS):
            par = n % 2
            wts = [gp_dm, ve_lamn] if n == 0 else []
            if sc_tt_prev[par] is not None:
                wts.append(sc_tt_prev[par])
            op("vector", lambda e, _n=n, _par=par: e.tensor_scalar(
                fa(0, 128, O_TSCR + 128 + _par * 128, 128),
                fa(0, 128, O_TSCR, 128),
                fa(0, 128, O_LAMN + _n, 1), None, ALU.mult),
               waits=wts, incs=[(VE, 1)])
            ve_tt = wv(VE)
            op("scalar", lambda e, _n=n, _par=par: e.activation(
                ba(0, 128, B_TTALL + _n * 128, 128),
                fa(0, 128, O_TSCR + 128 + _par * 128, 128), AF.Exp),
               waits=[ve_tt], incs=[(SC, 1)])
            sc_tt_prev[par] = wv(SC)
        sc_tt = wv(SC)

        op("tensor", lambda e: e.matmul(PD.t[:, 16:17], sel16,
                                        fa(0, NS, O_LAMC, 1),
                                        start=True, stop=True),
           waits=[PD.free, sc_lamc], incs=[(TN, 1)])
        evac("vector", PD, lambda e: e.tensor_scalar(fa(0, 128, O_LAMT, 1),
                                                     PD.t[:, 16:17], -1.0,
                                                     None, ALU.mult),
             waits=[wv(TN), wv(VE)])
        ve_lamt = wv(VE)

        dma("gpsimd", GU, ccin.ap(), ba(0, NS, B_UFM, T), waits=[sc_u])
        gu_ccin = wv(GU)
        op("gpsimd", lambda e: e.collective_compute(
            "AllGather", ALU.bypass, replica_groups=[list(range(NCORES))],
            ins=[ccin.ap().opt()], outs=[ccout.ap().opt()]),
           waits=[wv(GU)], incs=[(CC, 1)])
        dma("gpsimd", GU, ba(0, 128, B_UALL, T), ccout.ap(), waits=[wv(CC)])
        gu_all = wv(GU)

        for c in range(SCH):
            p = PA if c % 2 == 0 else PB
            op("tensor", lambda e, _c=c, _p=p: e.matmul(
                _p.t[:, 0:NS], ba(0, NS, B_UFM + _c * 128, 128), id16b,
                start=True, stop=True),
               waits=[p.free] + ([sc_u] if c == 0 else []), incs=[(TN, 1)])
            tn_t = wv(TN)
            evac("vector", p, lambda e, _c=c, _p=p: e.tensor_copy(
                ba(0, 128, B_TALL + _c * 16, 16), _p.t[:, 0:NS]),
                 waits=[tn_t])
        ve_tall = wv(VE)
        tn_utrans = wv(TN)
        for n in range(NS):
            mm(PC, PC.t[:, n * SCH:(n + 1) * SCH],
               ba(0, 128, B_TTALL + n * 128, 128),
               ABF[0:128, B_TALL + n:B_TALL + 16 * SCH:16], True, True,
               waits=([PC.free, ve_tall, sc_tt] if n == 0 else ()),
               inc=(n == NS - 1))
        tn_scan = wv(TN)
        evac("vector", PC, lambda e: e.tensor_copy(
            fa(0, 128, O_HLOC, 16 * SCH), PC.t[:, 0:16 * SCH]),
             waits=[tn_scan])
        ve_hloc = wv(VE)
        for c in range(SCH):
            p = PA if c % 2 == 0 else PB
            op("tensor", lambda e, _c=c, _p=p: e.matmul(
                _p.t[0:NS, 0:128],
                AF32[0:128, O_HLOC + _c:O_HLOC + 16 * SCH:SCH],
                id128f, start=True, stop=True),
               waits=[p.free] + ([ve_hloc] if c == 0 else []),
               incs=[(TN, 1)])
            tn_hb = wv(TN)
            evac("vector", p, lambda e, _c=c, _p=p: e.tensor_copy(
                fa(0, NS, O_HFM + _c * 128, 128), _p.t[0:NS, 0:128]),
                 waits=[tn_hb])
        ve_hfm = wv(VE)

        dma("sync", SX, ba(0, 128, B_CEXP, T), ce_h, waits=[wv(TN)])
        sx_ce = wv(SX)
        op("vector", lambda e: e.tensor_scalar(fa(0, 128, O_TSCR, T),
                                               ba(0, 128, B_CEXP, T),
                                               fa(0, 128, O_LAMT, 1), None,
                                               ALU.mult),
           waits=[ve_lamt, sx_ce, sc_tt], incs=[(VE, 1)])
        op("scalar", lambda e: e.activation(fa(0, 128, O_TSCR, T),
                                            fa(0, 128, O_TSCR, T), AF.Exp),
           waits=[wv(VE)], chain=True, incs=[(SC, 1)])
        op("vector", lambda e: e.tensor_mul(fa(0, 128, O_TSCR, T),
                                            fa(0, 128, O_TSCR, T),
                                            ba(0, 128, B_UALL, T)),
           waits=[wv(SC), gu_all], chain=True)
        op("vector", lambda e: e.reduce_sum(fa(0, 128, O_RSUM, 1),
                                            fa(0, 128, O_TSCR, T), AX.X),
           chain=True, incs=[(VE, 1)])
        ve_rsum = wv(VE)
        op("tensor", lambda e: e.matmul(PD.t[0:NS, 20:21], selt,
                                        fa(0, 128, O_RSUM, 1),
                                        start=True, stop=True),
           waits=[PD.free, ve_rsum], incs=[(TN, 1)])
        evac("vector", PD, lambda e: e.tensor_copy(fa(0, NS, O_CARRY, 1),
                                                   PD.t[0:NS, 20:21]),
             waits=[wv(TN)])
        op("vector", lambda e: e.tensor_copy(fa(0, NS, O_EVEC, 1),
                                             fa(0, NS, O_CARRY, 1)),
           chain=True)
        for c in range(1, SCH + 1):
            op("vector", lambda e, _c=c: e.scalar_tensor_tensor(
                fa(0, NS, O_EVEC + _c, 1), fa(0, NS, O_EVEC + _c - 1, 1),
                fa(0, NS, O_A128, 1),
                fa(0, NS, O_HFM + (_c - 1) * 128 + 127, 1),
                ALU.mult, ALU.add),
               waits=([sc_a128, ve_hfm] if c == 1 else ()), chain=True)
        for c in range(SCH):
            op("vector", lambda e, _c=c: e.scalar_tensor_tensor(
                fa(0, NS, O_HFM + _c * 128, 128), fa(0, NS, O_PPOW, 128),
                fa(0, NS, O_EVEC + _c, 1), fa(0, NS, O_HFM + _c * 128, 128),
                ALU.mult, ALU.add),
               waits=([sc_ppow] if c == 0 else ()), chain=True)
        op("vector", lambda e: e.tensor_copy(ba(0, NS, B_HBF, T),
                                             fa(0, NS, O_HFM, T)),
           chain=True, incs=[(VE, 1)])
        ve_hbf = wv(VE)
        if stop_after == "scan":
            dma("gpsimd", GU, out_h[0:128, 0:T], ba(0, 128, B_UALL, T),
                waits=[ve_hbf])
            dma("sync", SO, out_h[128:144, 0:T], fa(0, NS, O_HFM, T),
                waits=[ve_hbf])
            dma("sync", SO, out_h[144:160, 0:SCH + 1],
                fa(0, NS, O_EVEC, SCH + 1), slow=True)
            dma("sync", SO, out_h[160:176, 0:1], fa(0, NS, O_CARRY, 1), slow=True)
            dma("sync", SO, out_h[176:304, 0:1], fa(0, 128, O_RSUM, 1), slow=True)
            dma("gpsimd", GU, out_h[304:320, 0:T], ba(0, NS, B_UFM, T))
            dma("sync", SO, out_h[320:336, 0:T], fa(0, NS, O_PPOW, T))
            dma("sync", SO, out_h[336:464, 0:T], fa(0, 128, O_TSCR, T))
            op("gpsimd", lambda e: e.wait_ge(SO.h, SO.n))
            op("gpsimd", lambda e: e.wait_ge(GU.h, GU.n))
            _bstate["stopped"] = True

        dma("gpsimd", GF, ba(0, 128, B_OUTW, 8192).rearrange("p (a c) -> p a c", a=8),
            ow_h.rearrange("(a p) c -> p a c", p=128),
            waits=sz_wr_all)
        gf_outw = wv(GF)
        zrd_tok = None
        for ti in range(NT):
            t0 = ti * TT
            wd = min(TT, T - t0)
            zw = list(sz_wr_all) + ([zrd_tok] if zrd_tok else
                                    [tn_utrans, gu_ccin])
            dma("sync", SZR,
                ba(0, 128, B_ZRD, ZSLAB).rearrange("p (a t) -> p a t", a=8)[:, :, 0:wd],
                zd.ap().rearrange("p (a t) -> p a t", a=8)[:, :, t0:t0 + wd],
                waits=zw)
            sz_rd = wv(SZR)
            for oc in range(8):
                p = PA if oc % 2 == 0 else PB
                wts = [p.free] + ([ve_hbf, gf_small] if (ti == 0 and oc == 0) else [])
                mm(p, p.t[:, 0:wd], ba(0, NS, B_DREG + oc * 128, 128),
                   ba(0, NS, B_HBF + t0, wd), True, True, waits=wts, inc=True)
                tn_d = wv(TN)
                evac("vector", p, lambda e, _oc=oc, _wd=wd, _p=p:
                     e.tensor_mul(ba(0, 128, B_YSZ + _oc * 512, _wd),
                                  _p.t[:, 0:_wd],
                                  ba(0, 128, B_ZRD + _oc * TT, _wd)),
                     waits=[tn_d, sz_rd])
            ve_ysz = wv(VE)
            zrd_tok = ve_ysz
            for oc in range(8):
                p = PC if oc % 2 == 0 else PD
                wts = [p.free, ve_ysz] + ([gf_outw] if (ti == 0 and oc == 0) else [])
                for kc in range(NCH):
                    mm(p, p.t[:, 0:wd],
                       ba(0, 128, B_OUTW + kc * 1024 + oc * 128, 128),
                       ba(0, 128, B_YSZ + kc * 512, wd),
                       kc == 0, kc == NCH - 1,
                       waits=(wts if kc == 0 else ()), inc=(kc == NCH - 1))
                tn_ow = wv(TN)
                evac("vector", p, lambda e, _oc=oc, _t0=t0, _wd=wd, _p=p:
                     e.scalar_tensor_tensor(
                         xbuf_ch(_oc, 3 + _t0, _wd), _p.t[:, 0:_wd],
                         vcol("out_b", _oc), xbuf_ch(_oc, 3 + _t0, _wd),
                         ALU.add, ALU.add),
                     waits=[tn_ow])
        ve_x3 = wv(VE)
        ckpt("x3", ve_x3)

        for kc in range(NCH):
            op("vector", lambda e, _kc=kc: e.tensor_copy(
                act_ch(_kc, 0, TH), xbuf_ch(_kc, 0, TH)),
               waits=([ve_x3] if kc == 0 else ()),
               incs=([(VE, 1)] if kc == NCH - 1 else []))
        ve_x3b = wv(VE)
        ve_st_2 = emit_stats(TS, ve_x3b)
        emit_affine(False, (3, T),
                    lambda kc: vcol("ln2_w", kc), lambda kc: vcol("ln2_b", kc),
                    first_wait=[ve_st_2])
        ve_h2 = wv(VE)

        wslot = {"i": 0}
        wb_last2 = [wv(TN), wv(TN)]

        def load_slice(src, extra_waits=()):
            i = wslot["i"] % 2
            wslot["i"] += 1
            sem = GWs[i]
            dma("gpsimd", sem, wbuf_view(i), src,
                waits=[wb_last2[i]] + list(extra_waits))
            return i, (sem, sem.n)

        def w1_slice(h, g):
            return h[:, g * 1024:(g + 1) * 1024].rearrange("(a p) c -> p a c", p=128)

        def w2_slice(h, g):
            return h[g * 1024:(g + 1) * 1024, :].rearrange("(a p) c -> p a c", p=128)

        def ffn_group(i1, tok1, i2, tok2, in_off_fn, masked, extra_mm,
                      bias_base, first_waits=()):
            gidx = _bstate.get("ffn_gidx", 0)
            _bstate["ffn_gidx"] = gidx + 1
            tpar = {0: None, 1: None}
            for ffc in range(8):
                p = PA if ffc % 2 == 0 else PB
                wts = [p.free, tok1] + (list(first_waits) if ffc == 0 else [])
                for s in range(NHB):
                    for kc in range(NCH):
                        mm(p, p.t[:, s * STP:s * STP + STP],
                           ba(0, 128, WBo[i1] + kc * 1024 + ffc * 128, 128),
                           in_off_fn(kc, s),
                           kc == 0, kc == NCH - 1,
                           waits=(wts if (kc == 0 and s == 0) else ()),
                           inc=(kc == NCH - 1 and s == NHB - 1))
                tn_a = wv(TN)
                if masked is None:
                    evac("scalar", p, lambda e, _f=ffc, _p=p:
                         e.activation(ba(0, 128, B_G1 + _f * Th, Th),
                                      _p.t[:, 0:Th], AF.Gelu,
                                      bias=fa(0, 128, O_VEC + bias_base + _f, 1)),
                         waits=[tn_a])
                else:
                    par = ffc % 2
                    toff = O_TSCR + par * Th
                    wts2 = [tn_a] + ([tpar[par]] if tpar[par] is not None else [])
                    evac("vector", p, lambda e, _f=ffc, _p=p, _to=toff:
                         e.scalar_tensor_tensor(
                             fa(0, 128, _to, Th), masked,
                             fa(0, 128, O_VEC + bias_base + _f, 1),
                             _p.t[:, 0:Th], ALU.mult, ALU.add),
                         waits=wts2)
                    op("scalar", lambda e, _f=ffc, _to=toff:
                       e.activation(ba(0, 128, B_G1 + _f * Th, Th),
                                    fa(0, 128, _to, Th), AF.Gelu),
                       waits=[wv(VE)], incs=[(SC, 1)])
                    tpar[par] = wv(SC)
                if gidx == 0 and stop_after == f"ffc{ffc}":
                    ckpt(f"ffc{ffc}", wv(SC))
                    return
            sc_a = wv(SC)
            if gidx == 0 and stop_after == "ffnA":
                ckpt("ffnA", sc_a)
                return
            for oc in range(8):
                p = PC if oc % 2 == 0 else PD
                wts = [p.free, sc_a, tok2]
                has_extra = bool(extra_mm)
                for s in range(NHB):
                    for ffc in range(8):
                        last = (ffc == 7 and s == NHB - 1)
                        mm(p, p.t[:, s * STP:s * STP + STP],
                           ba(0, 128, WBo[i2] + ffc * 1024 + oc * 128, 128),
                           ba(0, 128, B_G1 + ffc * Th + s * STP, STP),
                           ffc == 0, ffc == 7 and not has_extra,
                           waits=(wts if (ffc == 0 and s == 0) else ()),
                           inc=(last and not has_extra))
                for em in extra_mm:
                    em(oc, p)
                if gidx == 0 and stop_after == f"ffnB{oc}":
                    ckpt(f"ffnB{oc}", wv(TN))
                    return
                yield oc, p, wv(TN)

        first = [ve_h2]
        for hh in range(2):
            for g in range(2):
                i1, tok1 = load_slice(w1_slice(m1_h, g))
                i2, tok2 = load_slice(w2_slice(m2_h, g))
                hof = 3 + hh * Th

                def mkin(_hof):
                    return lambda kc, s: act_ch(kc, _hof + s * STP, STP)
                gen = ffn_group(i1, tok1, i2, tok2, mkin(hof), None, [],
                                VP["mlp_b1"] + g * 8, first_waits=first)
                first = []
                for oc, p, tn_b in gen:
                    if g == 1:
                        evac("vector", p, lambda e, _oc=oc, _h=hof, _p=p:
                             e.scalar_tensor_tensor(
                                 xbuf_ch(_oc, _h, Th), _p.t[:, 0:Th],
                                 vcol("mlp_b2", _oc), xbuf_ch(_oc, _h, Th),
                                 ALU.add, ALU.add),
                             waits=[tn_b])
                    else:
                        evac("vector", p, lambda e, _oc=oc, _h=hof, _p=p:
                             e.scalar_tensor_tensor(
                                 xbuf_ch(_oc, _h, Th), _p.t[:, 0:Th], 0.0,
                                 xbuf_ch(_oc, _h, Th), ALU.add, ALU.add),
                             waits=[tn_b])
                    if stop_after == f"ffnE{oc}" and hh == 0 and g == 0:
                        ckpt(f"ffnE{oc}", wv(VE))
                        break
                wb_last2[i1] = wv(TN)
                wb_last2[i2] = wv(TN)
                if stop_after == "mlp0" and hh == 0 and g == 0:
                    ckpt("mlp0", wv(VE))
        ve_mlp = wv(VE)
        ckpt("mlp", ve_mlp)

        comb = []
        nhalf = max(1, T // 1024)
        hw2 = min(1024, T)
        for kc in range(NCH):
            for hf in range(nhalf):
                idx = kc * nhalf + hf
                slot = O_TSCR + (idx % 2) * hw2
                psem = SXA if idx % 2 == 0 else SXB
                wts = ([comb[idx - 2]] if idx >= 2 else [ve_mlp])
                dma("sync", psem, fa(0, 128, slot, hw2),
                    x_h[kc * 128:(kc + 1) * 128,
                        3 + hf * hw2:3 + hf * hw2 + hw2],
                    waits=wts)
                sx_c = wv(psem)
                op("vector", lambda e, _kc=kc, _hf=hf, _s=slot:
                   e.scalar_tensor_tensor(
                       xbuf_ch(_kc, 3 + _hf * hw2, hw2),
                       xbuf_ch(_kc, 3 + _hf * hw2, hw2),
                       fa(0, 128, O_GACOL + _kc, 1), fa(0, 128, _s, hw2),
                       ALU.mult, ALU.add),
                   waits=[sx_c, ve_cols] + ([ve_mlp] if idx == 0 else []),
                   incs=[(VE, 1)])
                comb.append(wv(VE))
        ve_xnew = wv(VE)
        ckpt("xnew", ve_xnew)

        for kc in range(NCH):
            op("vector", lambda e, _kc=kc: e.tensor_copy(
                act_ch(_kc, 0, TH), xbuf_ch(_kc, 0, TH)),
               waits=([ve_xnew] if kc == 0 else ()),
               incs=([(VE, 1)] if kc == NCH - 1 else []))
        ve_xnb = wv(VE)
        ve_st_bb = emit_stats(TS, ve_xnb)
        emit_affine(False, (3, T),
                    lambda kc: fa(0, 128, O_BCOL + kc, 1),
                    lambda kc: fa(0, 128, O_MCOL + 24 + kc, 1),
                    first_wait=[ve_st_bb])
        ve_hb = wv(VE)

        for ti in range(NT):
            t0 = 3 + ti * TT
            wd = min(TT, T - ti * TT)
            wts = [PA.free] + ([ve_hb, gf_small] if ti == 0 else [])
            for kc in range(NCH):
                mm(PA, PA.t[0:4, 0:wd], ba(0, 128, B_WGB + kc * 4, 4),
                   act_ch(kc, t0, wd), kc == 0, kc == NCH - 1,
                   waits=(wts if kc == 0 else ()), inc=(kc == NCH - 1))
            tn_g = wv(TN)
            evac("scalar", PA, lambda e, _ti=ti, _wd=wd: e.add(
                fa(0, 4, O_GATE + _ti * TT, _wd), PA.t[0:4, 0:_wd],
                fa(0, 4, O_VEC + VP["bg"], 1)),
                 waits=[tn_g])
        sc_gate = wv(SC)
        for c in range(NTC):
            op("tensor", lambda e, _c=c: e.matmul(
                PB.t[0:128, 0:4], fa(0, 4, O_GATE + _c * 128, 128), id4f,
                start=True, stop=True),
               waits=[PB.free] + ([sc_gate] if c == 0 else []),
               incs=[(TN, 1)])
            tn_gt = wv(TN)
            op("vector", lambda e: e.tensor_copy(fa(0, 128, O_GTM, 4),
                                                 PB.t[0:128, 0:4]),
               waits=[tn_gt], chain=True, incs=[(VE, 1)])
            PB.free = wv(VE)
            op("vector", lambda e: e.reduce_max(fa(0, 128, O_GMX, 1),
                                                fa(0, 128, O_GTM, 4), AX.X),
               chain=True)
            for ei in range(4):
                op("vector", lambda e, _e=ei: e.tensor_tensor(
                    fa(0, 128, O_GMX + 1 + _e, 1), fa(0, 128, O_GTM + _e, 1),
                    fa(0, 128, O_GMX, 1), ALU.is_ge), chain=True)
            op("vector", lambda e, _c=c: e.tensor_copy(
                fa(0, 128, O_SELTM + _c * 4, 1), fa(0, 128, O_GMX + 1, 1)),
               chain=True)
            op("vector", lambda e: e.tensor_scalar(
                fa(0, 128, O_GMX + 5, 1), fa(0, 128, O_GMX + 1, 1), -1.0, 1.0,
                ALU.mult, ALU.add), chain=True)
            op("vector", lambda e, _c=c: e.tensor_mul(
                fa(0, 128, O_SELTM + _c * 4 + 1, 1), fa(0, 128, O_GMX + 2, 1),
                fa(0, 128, O_GMX + 5, 1)), chain=True)
            op("vector", lambda e, _c=c: e.tensor_sub(
                fa(0, 128, O_GMX + 6, 1), fa(0, 128, O_GMX + 5, 1),
                fa(0, 128, O_SELTM + _c * 4 + 1, 1)), chain=True)
            op("vector", lambda e, _c=c: e.tensor_mul(
                fa(0, 128, O_SELTM + _c * 4 + 2, 1), fa(0, 128, O_GMX + 3, 1),
                fa(0, 128, O_GMX + 6, 1)), chain=True)
            op("vector", lambda e, _c=c: e.tensor_sub(
                fa(0, 128, O_SELTM + _c * 4 + 3, 1), fa(0, 128, O_GMX + 6, 1),
                fa(0, 128, O_SELTM + _c * 4 + 2, 1)),
               chain=True, incs=[(VE, 1)])
        ve_sel = wv(VE)
        for c in range(NTC):
            op("tensor", lambda e, _c=c: e.matmul(
                PC.t[0:4, 0:128], fa(0, 128, O_SELTM + _c * 4, 4), id128f,
                start=True, stop=True),
               waits=[PC.free] + ([ve_sel] if c == 0 else []),
               incs=[(TN, 1)])
            tn_m = wv(TN)
            evac("vector", PC, lambda e, _c=c: e.tensor_copy(
                ba(0, 4, B_MASK4 + _c * 128, 128), PC.t[0:4, 0:128]),
                 waits=[tn_m])
        ve_mask4 = wv(VE)
        mask4 = lambda off, w: ba(0, 4, B_MASK4 + off, w)
        if stop_after == "masks":
            dma("gpsimd", GU, out_h[0:4, 0:T], ba(0, 4, B_MASK4, T),
                waits=[ve_mask4])
            dma("sync", SO, out_h[8:12, 0:T], fa(0, 4, O_GATE, T),
                waits=[ve_mask4])
            dma("sync", SO, out_h[16:144, 0:4 * NTC], fa(0, 128, O_SELTM, 4 * NTC),
                slow=True)
            dma("sync", SO, out_h[144:272, 0:T], fa(0, 128, O_TSCR, T))
            dma("sync", SO, out_h[272:400, 0:48], fa(0, 128, O_MCOL, 48),
                slow=True)
            dma("sync", SO, out_h[400:528, 0:40], fa(0, 128, O_ACOL, 40),
                slow=True)
            op("gpsimd", lambda e: e.wait_ge(SO.h, SO.n))
            op("gpsimd", lambda e: e.wait_ge(GU.h, GU.n))
            _bstate["stopped"] = True

        dma("gpsimd", GF, ba(0, NE, B_DREG, DM), b2m_h, waits=[wv(TN)])
        gf_b2m = wv(GF)

        def m2_mm(hh):
            def add(oc, p):
                for s in range(NHB):
                    mm(p, p.t[:, s * STP:s * STP + STP],
                       ba(0, 4, B_DREG + oc * 128, 128),
                       mask4(hh * Th + s * STP, STP),
                       False, True,
                       waits=([gf_b2m] if (oc == 0 and s == 0) else ()),
                       inc=(s == NHB - 1))
            return add

        for hh in range(2):
            for ei in range(NE):
                wts = [PA.free, ve_mask4]
                for s in range(NHB):
                    mm(PA, PA.t[:, s * STP:s * STP + STP],
                       ba(0, 4, B_CBF + CB_SELROW + ei * 128, 128),
                       mask4(hh * Th + s * STP, STP), True, True,
                       waits=(wts if s == 0 else ()), inc=(s == NHB - 1))
                tn_mbc = wv(TN)
                evac("vector", PA, lambda e: e.tensor_copy(
                    ba(0, 128, B_MBC, Th), PA.t[:, 0:Th]),
                     waits=[tn_mbc])
                ve_mbc = wv(VE)
                for kc in range(NCH):
                    op("vector", lambda e, _kc=kc, _hh=hh: e.tensor_mul(
                        ba(0, 128, B_HMI + _kc * Th, Th),
                        act_ch(_kc, 3 + _hh * Th, Th),
                        ba(0, 128, B_MBC, Th)),
                       waits=([ve_mbc] if kc == 0 else ()),
                       incs=([(VE, 1)] if kc == NCH - 1 else []))
                ve_hmi = wv(VE)
                for g in range(4):
                    i1, tok1 = load_slice(w1_slice(f1_h[ei], g))
                    i2, tok2 = load_slice(w2_slice(f2_h[ei], g))
                    extra = [m2_mm(hh)] if (ei == 0 and g == 0) else []

                    def mkin2():
                        return lambda kc, s: ba(0, 128, B_HMI + kc * Th + s * STP, STP)
                    gen = ffn_group(i1, tok1, i2, tok2, mkin2(),
                                    ba(0, 128, B_MBC, Th), extra,
                                    VP["b1"] + ei * 32 + g * 8,
                                    first_waits=([ve_hmi] if g == 0 else []))
                    for oc, p, tn_b in gen:
                        evac("vector", p, lambda e, _oc=oc, _hh=hh, _p=p:
                             e.scalar_tensor_tensor(
                                 xbuf_ch(_oc, 3 + _hh * Th, Th), _p.t[:, 0:Th],
                                 fa(0, 128, O_GBCOL + _oc, 1),
                                 xbuf_ch(_oc, 3 + _hh * Th, Th),
                                 ALU.mult, ALU.add),
                             waits=[tn_b])
                    wb_last2[i1] = wv(TN)
                    wb_last2[i2] = wv(TN)
            ve_half = wv(VE)
            for oc in range(NCH):
                dma("sync", SO, out_h[oc * 128:(oc + 1) * 128,
                                      hh * Th:hh * Th + Th],
                    xbuf_ch(oc, 3 + hh * Th, Th),
                    waits=([ve_half] if oc == 0 else []))
        so_all = wv(SO)
        op("gpsimd", lambda e: e.wait_ge(SO.h, so_all[1]))

        with nc.Block() as block:
            @block.sync
            def _(e):
                for f in Q["sync"]:
                    f(e)

            @block.gpsimd
            def _(e):
                for f in Q["gpsimd"]:
                    f(e)

            @block.tensor
            def _(e):
                for f in Q["tensor"]:
                    f(e)

            @block.vector
            def _(e):
                for f in Q["vector"]:
                    f(e)

            @block.scalar
            def _(e):
                for f in Q["scalar"]:
                    f(e)

    return nc


LAYOUT = {}
_PROG_CACHE = {}


def get_program(T, stop_after=None):
    key = (T, stop_after)
    if key not in _PROG_CACHE:
        _PROG_CACHE[key] = build_program(T, stop_after)
    return _PROG_CACHE[key]


def kernel(**inputs):
    S = np.asarray(inputs["L0_attn_x"]).shape[1]
    T = S // NCORES
    in_maps = host_pack(inputs, T)
    nc = get_program(T)
    res = run_bass_kernel_spmd(nc, in_maps, core_ids=list(range(NCORES)))
    parts = [np.asarray(res.results[k]["out"]) for k in range(NCORES)]
    full = np.concatenate(parts, axis=1)
    return np.ascontiguousarray(full.T)[None, :, :].astype(np.float32)


# version 03 (final)
# speedup vs baseline: 1.0593x; 1.0593x over previous
import numpy as np
import ml_dtypes
from contextlib import ExitStack

import concourse.bass as bass
import concourse.mybir as mybir
from concourse.bass_utils import run_bass_kernel_spmd

F32 = mybir.dt.float32
BF16 = mybir.dt.bfloat16
AF = mybir.ActivationFunctionType
ALU = mybir.AluOpType
AX = mybir.AxisListType

NCORES = 8
DM = 1024
NCH = 8
NS = 16
NE = 4
LN_EPS = 1e-5

VP = {}
_c = 0
for _nm, _n in [("ln1_w", 8), ("ln1_b", 8), ("lns_w", 8), ("lns_b", 8),
                ("ln2_w", 8), ("ln2_b", 8), ("conv_b", 8), ("out_b", 8),
                ("mlp_b2", 8), ("in_b", 16), ("mlp_b1", 16),
                ("xp_b", 1), ("A_log", 1), ("bg", 1), ("c", 8),
                ("b1", 128), ("b2", 32)]:
    VP[_nm] = _c
    _c += _n
NV = _c

C_ID = 0
C_SELT = 128
C_SEL16 = 144
C_ONEROW = 272
NC32 = 400
CB_ONECOL = 0
CB_ONEROW = 1
CB_ID = 129
CB_SELROW = 257
NCB = 257 + 512


def host_pack(inputs, T):
    d = inputs
    x = np.asarray(d["L0_attn_x"])[0]
    S = x.shape[0]
    assert S == NCORES * T
    xf = np.ascontiguousarray(np.asarray(x, np.float32).T)
    xpad = np.concatenate([np.zeros((DM, 3), np.float32), xf], axis=1)

    vec = np.zeros((128, NV), np.float32)

    def put(name, arr):
        a = np.asarray(arr, np.float32).reshape(-1)
        n = a.size // 128
        vec[:, VP[name]:VP[name] + n] = a.reshape(n, 128).T

    def put_small(name, arr):
        a = np.asarray(arr, np.float32).reshape(-1)
        vec[:a.size, VP[name]] = a

    put("ln1_w", d["L0_attn_ln1_w"]); put("ln1_b", d["L0_attn_ln1_b"])
    put("lns_w", d["L0_attn_lns_w"]); put("lns_b", d["L0_attn_lns_b"])
    put("ln2_w", d["L0_attn_ln2_w"]); put("ln2_b", d["L0_attn_ln2_b"])
    put("conv_b", d["L0_attn_conv_b"]); put("out_b", d["L0_attn_out_b"])
    put("mlp_b2", d["L0_attn_mlp_b2"]); put("in_b", d["L0_attn_in_b"])
    put("mlp_b1", d["L0_attn_mlp_b1"])
    put_small("xp_b", d["L0_attn_xp_b"]); put_small("A_log", d["L0_attn_A_log"])
    put_small("bg", d["L0_ffn_bg"])
    put("c", d["L0_c"]); put("b1", d["L0_ffn_b1"]); put("b2", d["L0_ffn_b2"])

    cf32 = np.zeros((128, NC32), np.float32)
    cf32[:, C_ID:C_ID + 128] = np.eye(128)
    for p in range(128):
        cf32[p, C_SELT + (p % 16)] = 1.0
    for k in range(128):
        cf32[k % 16, C_SEL16 + k] = 1.0
    cf32[0, C_ONEROW:C_ONEROW + 128] = 1.0

    cbf = np.zeros((128, NCB), ml_dtypes.bfloat16)
    cbf[:, CB_ONECOL] = 1.0
    cbf[0, CB_ONEROW:CB_ONEROW + 128] = 1.0
    cbf[:, CB_ID:CB_ID + 128] = np.eye(128, dtype=ml_dtypes.bfloat16)
    for e in range(4):
        cbf[e, CB_SELROW + e * 128:CB_SELROW + (e + 1) * 128] = 1.0

    alog_row = np.asarray(d["L0_attn_A_log"], np.float32).reshape(1, NS)
    conv_w = np.asarray(d["L0_attn_conv_w"], np.float32).reshape(DM, 4)
    convw = np.zeros((128, 32), np.float32)
    for j in range(4):
        for kc in range(NCH):
            convw[:, j * 8 + kc] = conv_w[kc * 128:(kc + 1) * 128, j]

    in_maps = []
    for k in range(NCORES):
        cexp = np.zeros((128, T), np.float64)
        T0 = k * T
        t = np.arange(T, dtype=np.float64)
        for j in range(NCORES):
            if j < k:
                cexp[j * 16:(j + 1) * 16, :] = ((T0 - 1.0) - (j * T + t))[None, :]
            else:
                cexp[j * 16:(j + 1) * 16, :] = 1e9
        m = {
            "x": np.ascontiguousarray(xpad[:, k * T:k * T + T + 3]),
            "hmask": np.full((128, 1), 0.0 if k == 0 else 1.0, np.float32),
            "vecpack": vec, "cf32": cf32, "cbf": cbf,
            "alog_row": alog_row, "convw": convw,
            "cexp": cexp.astype(ml_dtypes.bfloat16),
            "in_w": np.asarray(d["L0_attn_in_w"], np.float32),
            "xp_w": np.asarray(d["L0_attn_xp_w"], np.float32),
            "d_mat": np.asarray(d["L0_attn_D"], np.float32),
            "out_w": np.asarray(d["L0_attn_out_w"], np.float32),
            "mlp_w1": np.asarray(d["L0_attn_mlp_w1"], np.float32),
            "mlp_w2": np.asarray(d["L0_attn_mlp_w2"], np.float32),
            "ffn_w1": np.asarray(d["L0_ffn_W1"], np.float32),
            "ffn_w2": np.asarray(d["L0_ffn_W2"], np.float32),
            "wg": np.asarray(d["L0_ffn_Wg"], np.float32),
            "b2m": np.asarray(d["L0_ffn_b2"], np.float32),
            "wmod": np.asarray(d["L0_wmod"], np.float32),
        }
        in_maps.append(m)
    return in_maps


class Sem:
    def __init__(self, stk, nc, name):
        self.h = stk.enter_context(nc.semaphore(name))
        self.n = 0


class PsumT:
    def __init__(self, handle):
        self.t = handle
        self.free = None


def build_program(T, stop_after=None):
    TH = T + 3
    SCH = T // 128
    Th = T // 2
    NT = max(1, T // 512)
    TT = min(512, T)
    NTC = T // 128
    NHB = max(1, Th // 512)
    STP = min(512, Th)
    THS = []
    c0 = 0
    while c0 < TH:
        w = min(512, TH - c0)
        THS.append((c0, w))
        c0 += w
    TS = [(3 + s * TT, min(TT, T - s * TT)) for s in range(NT)]

    W_TSCR = max(TH, 2051)
    O_XBUF = 0
    O_TSCR = O_XBUF + NCH * TH
    O_ROWS = O_TSCR + W_TSCR
    O_PPOW = O_ROWS
    O_HFM = O_ROWS + T
    O_GATE = O_ROWS
    O_CST = O_ROWS + 2 * TH
    O_VEC = O_CST
    O_CF32 = O_VEC + NV
    O_CW = O_CF32 + NC32
    O_MCOL = O_CW + 32
    O_ACOL = O_MCOL + 48
    O_BCOL = O_ACOL + 8
    O_GACOL = O_BCOL + 8
    O_GBCOL = O_GACOL + 8
    O_CLAMP = O_GBCOL + 8
    O_GMX = O_CLAMP + 8
    O_GTM = O_GMX + 8
    O_HM = O_GTM + 4
    O_SELTM = O_HM + 1
    O_HLOC = O_SELTM + 4 * NTC
    O_EVEC = O_HLOC + 16 * SCH
    O_CARRY = O_EVEC + (SCH + 1)
    O_A128 = O_CARRY + 1
    O_LAMC = O_A128 + 1
    O_LAMT = O_LAMC + 1
    O_RSUM = O_LAMT + 1
    O_LAMN = O_RSUM + 1
    O_LROW = O_LAMN + 16
    O_ALR = O_LROW + 16
    NF32 = O_ALR + 16

    B_ACT = 0
    B_WA = B_ACT + NCH * TH
    B_WB = B_WA + 8192
    B_X = B_WB + 8192
    B_SA = B_X
    B_SB = B_X + 4096
    B_SC = B_X + 8192
    B_SD = B_X + 12288
    ZSLAB = 8 * TT
    RBW = max(2 * TH, ZSLAB)
    B_RB = B_X + 16384
    B_MRB = B_RB + TH
    B_DREG = B_RB + RBW
    B_CBF = B_DREG + 1024
    B_XPW = B_CBF + NCB
    B_WGB = B_XPW + 128
    B_TALL = B_WGB + 32
    B_RMR = B_TALL + 16 * SCH
    NBF = B_RMR + 32

    B_XPT = B_SC
    B_ZW0, B_ZW1 = B_SA, B_SB
    B_OUTW = B_SA
    B_TTALL = B_SC
    B_HBF = B_SC + 2048
    B_YSZ = B_SD
    B_CEXP = B_SD
    B_UALL = B_SD + 2048
    B_G1 = B_SA
    B_XSQ = B_SD
    B_HMI = B_SC
    B_UFM = B_RB
    B_ZRD = B_RB
    B_MASK4 = B_RB
    B_MBC = B_RB + T

    global LAYOUT
    LAYOUT = {k: v for k, v in locals().items() if k.startswith(("O_", "B_", "N"))}
    nc = bass.Bass("TRN2", target_bir_lowering=False, detect_race_conditions=False)

    x_h = nc.dram_tensor("x", [DM, TH], F32, kind="ExternalInput").ap()
    vp_h = nc.dram_tensor("vecpack", [128, NV], F32, kind="ExternalInput").ap()
    cf_h = nc.dram_tensor("cf32", [128, NC32], F32, kind="ExternalInput").ap()
    cb_h = nc.dram_tensor("cbf", [128, NCB], BF16, kind="ExternalInput").ap()
    al_h = nc.dram_tensor("alog_row", [1, NS], F32, kind="ExternalInput").ap()
    hm_h = nc.dram_tensor("hmask", [128, 1], F32, kind="ExternalInput").ap()
    cw_h = nc.dram_tensor("convw", [128, 32], F32, kind="ExternalInput").ap()
    ce_h = nc.dram_tensor("cexp", [128, T], BF16, kind="ExternalInput").ap()
    inw_h = nc.dram_tensor("in_w", [DM, 2048], F32, kind="ExternalInput").ap()
    xpw_h = nc.dram_tensor("xp_w", [DM, NS], F32, kind="ExternalInput").ap()
    dm_h = nc.dram_tensor("d_mat", [NS, DM], F32, kind="ExternalInput").ap()
    ow_h = nc.dram_tensor("out_w", [DM, DM], F32, kind="ExternalInput").ap()
    m1_h = nc.dram_tensor("mlp_w1", [DM, 2048], F32, kind="ExternalInput").ap()
    m2_h = nc.dram_tensor("mlp_w2", [2048, DM], F32, kind="ExternalInput").ap()
    f1_h = nc.dram_tensor("ffn_w1", [NE, DM, 4096], F32, kind="ExternalInput").ap()
    f2_h = nc.dram_tensor("ffn_w2", [NE, 4096, DM], F32, kind="ExternalInput").ap()
    wg_h = nc.dram_tensor("wg", [DM, NE], F32, kind="ExternalInput").ap()
    b2m_h = nc.dram_tensor("b2m", [NE, DM], F32, kind="ExternalInput").ap()
    wm_h = nc.dram_tensor("wmod", [DM, 6 * DM], F32, kind="ExternalInput").ap()
    out_h = nc.dram_tensor("out", [DM, T], F32, kind="ExternalOutput").ap()
    ccin = nc.dram_tensor("ccin", [NS, T], BF16, kind="Internal")
    ccout = nc.dram_tensor("ccout", [128, T], BF16, kind="Internal")
    zd = nc.dram_tensor("zd", [128, NCH * T], BF16, kind="Internal")

    stk = ExitStack()
    with stk:
        AF32 = stk.enter_context(nc.sbuf_tensor("af32", [128, NF32], F32))
        ABF = stk.enter_context(nc.sbuf_tensor("abf", [128, NBF], BF16))

        def fa(p0, pn, o, w):
            return AF32[p0:p0 + pn, o:o + w]

        def ba(p0, pn, o, w):
            return ABF[p0:p0 + pn, o:o + w]

        PA = PsumT(stk.enter_context(nc.psum_tensor("PA", [128, 1024], F32)))
        PB = PsumT(stk.enter_context(nc.psum_tensor("PB", [128, 1024], F32)))
        PC = PsumT(stk.enter_context(nc.psum_tensor("PC", [128, 1024], F32)))
        PD = PsumT(stk.enter_context(nc.psum_tensor("PD", [128, 1024], F32)))

        SM = Sem(stk, nc, "sm")
        SMA = Sem(stk, nc, "sma")
        SMB = Sem(stk, nc, "smb")
        SX = Sem(stk, nc, "sx")
        SXA = Sem(stk, nc, "sxa")
        SXB = Sem(stk, nc, "sxb")
        SZE = Sem(stk, nc, "sze")
        SZO = Sem(stk, nc, "szo")
        SZR = Sem(stk, nc, "szr")
        SO = Sem(stk, nc, "so")
        G1 = Sem(stk, nc, "g1")
        GF = Sem(stk, nc, "gf")
        GW0 = Sem(stk, nc, "gw0")
        GW1 = Sem(stk, nc, "gw1")
        GU = Sem(stk, nc, "gu")
        CC = Sem(stk, nc, "cc")
        GP = Sem(stk, nc, "gp")
        TN = Sem(stk, nc, "tn")
        VE = Sem(stk, nc, "ve")
        SC = Sem(stk, nc, "sc")
        GWs = [GW0, GW1]
        WBo = [B_WA, B_WB]
        _ENG_SEM_INIT = True

        Q = {k: [] for k in ("sync", "gpsimd", "tensor", "vector", "scalar")}
        _bstate = {"stopped": False}

        ENG_SEM = {}

        def op(eng, fn, waits=(), incs=(), chain=False):
            if _bstate["stopped"]:
                return
            waits = [w for w in waits if w is not None]
            if chain:
                s = ENG_SEM[eng]
                waits.append((s, s.n))
                incs = list(incs) + [(s, 1)]
            merged = {}
            for s, k in incs:
                merged[id(s)] = (s, merged.get(id(s), (s, 0))[1] + k)
            incs = list(merged.values())
            rincs = []
            for s, k in incs:
                s.n += k
                rincs.append((s, k))

            def run(e, _fn=fn, _w=waits, _i=rincs):
                for s, v in _w:
                    e.wait_ge(s.h, v)
                inst = _fn(e)
                for s, k in _i:
                    if inst is None:
                        inst = e.nop()
                    inst = inst.then_inc(s.h, k)
            Q[eng].append(run)

        ENG_SEM.update({"vector": VE, "scalar": SC, "tensor": TN,
                        "gpsimd": GP})

        def wv(sem):
            return (sem, sem.n)

        def dma(eng, sem, out, in_, waits=(), slow=False):
            op(eng, lambda e, o=out, i=in_, s=slow: e.dma_start(
                out=o, in_=i, allow_slow_non_contiguous=s),
               waits=waits, incs=[(sem, 16)])

        def mm(p, out_sl, lhsT, rhs, start, stop, waits=(), inc=False):
            op("tensor", lambda e, _o=out_sl, _l=lhsT, _r=rhs, _s=start, _p=stop:
               e.matmul(_o, _l, _r, start=_s, stop=_p),
               waits=waits, incs=([(TN, 1)] if inc else []))

        def evac(eng_name, p, fn, waits=(), last=True):
            sem = VE if eng_name == "vector" else SC
            op(eng_name, fn, waits=waits, incs=[(sem, 1)])
            if last and p is not None:
                p.free = wv(sem)
            return wv(sem)

        def ckpt(name, tok, src_off=None):
            if stop_after != name or _bstate["stopped"]:
                return
            so = O_XBUF if src_off is None else src_off
            for oc in range(NCH):
                dma("sync", SO, out_h[oc * 128:(oc + 1) * 128, 0:T],
                    fa(0, 128, so + oc * TH + 3, T),
                    waits=([tok] if oc == 0 else []))
            op("gpsimd", lambda e: e.wait_ge(SO.h, SO.n))
            _bstate["stopped"] = True

        onecol = ba(0, 128, B_CBF + CB_ONECOL, 1)
        onerow_b = ba(0, 1, B_CBF + CB_ONEROW, 128)
        onerow_f = fa(0, 1, O_CF32 + C_ONEROW, 128)
        id128f = fa(0, 128, O_CF32 + C_ID, 128)
        id16b = ba(0, NS, B_CBF + CB_ID, NS)
        id4f = fa(0, 4, O_CF32 + C_ID, 4)
        selt = fa(0, 128, O_CF32 + C_SELT, NS)
        sel16 = fa(0, NS, O_CF32 + C_SEL16, 128)

        def vcol(name, i=0):
            return fa(0, 128, O_VEC + VP[name] + i, 1)

        def act_ch(kc, off, w):
            return ba(0, 128, B_ACT + kc * TH + off, w)

        def xbuf_ch(kc, off, w):
            return fa(0, 128, O_XBUF + kc * TH + off, w)

        dma("sync", SM, fa(0, 128, O_VEC, NV), vp_h)
        dma("sync", SM, fa(0, 128, O_CF32, NC32), cf_h)
        dma("sync", SM, ba(0, 128, B_CBF, NCB), cb_h)
        dma("sync", SM, fa(0, 1, O_ALR, NS), al_h)
        dma("sync", SM, fa(0, 128, O_CW, 32), cw_h)
        dma("sync", SM, fa(0, 128, O_HM, 1), hm_h)
        sm_all = wv(SM)

        for c in range(NCH):
            dma("gpsimd", G1, act_ch(c, 0, TH), x_h[c * 128:(c + 1) * 128, :])
        g1_all = wv(G1)

        def wmod_src(sec):
            return wm_h[:, sec * DM:(sec + 1) * DM].rearrange(
                "(a p) c -> p a c", p=128)

        def wbuf_view(i):
            return ba(0, 128, WBo[i], 8192).rearrange("p (a c) -> p a c", a=8)

        dma("gpsimd", GF, ba(0, 128, B_XPW, 128).rearrange("p (a c) -> p a c", a=8),
            xpw_h.rearrange("(a p) c -> p a c", p=128))
        dma("gpsimd", GF, ba(0, NS, B_DREG, DM), dm_h)
        dma("gpsimd", GF, ba(0, 128, B_WGB, 32).rearrange("p (a c) -> p a c", a=8),
            wg_h.rearrange("(a p) c -> p a c", p=128))
        gf_small = wv(GF)

        op("scalar", lambda e: e.activation(fa(0, 128, O_CLAMP, 8),
                                            fa(0, 128, O_VEC + VP["c"], 8),
                                            AF.Silu),
           waits=[sm_all], incs=[(SC, 1)])
        sc_silc = wv(SC)
        prev_chunk_tn = {}
        for sec in range(6):
            for kc in range(8):
                gi = sec * 8 + kc
                slot = O_XBUF + (gi % 2) * 1024
                p = PC if gi % 2 == 0 else PD
                psem = SMA if gi % 2 == 0 else SMB
                lw = [prev_chunk_tn[gi - 2]] if gi >= 2 else []
                dma("sync", psem, fa(0, 128, slot, 1024),
                    wm_h[kc * 128:(kc + 1) * 128, sec * DM:(sec + 1) * DM],
                    waits=lw)
                sm_c = wv(psem)
                wts0 = [sm_c, sc_silc, p.free]
                for oc in range(8):
                    mm(p, p.t[:, oc:oc + 1],
                       fa(0, 128, slot + oc * 128, 128),
                       fa(0, 128, O_CLAMP + kc, 1),
                       True, True,
                       waits=(wts0 if oc == 0 else ()),
                       inc=(oc == 7))
                prev_chunk_tn[gi] = wv(TN)
                ve_pr = wv(VE)
                if kc == 0:
                    evac("vector", p, lambda e, _s=sec, _p=p:
                         e.tensor_copy(fa(0, 128, O_MCOL + _s * 8, 8),
                                       _p.t[:, 0:8]),
                         waits=[prev_chunk_tn[gi], ve_pr])
                else:
                    evac("vector", p, lambda e, _s=sec, _p=p:
                         e.tensor_add(fa(0, 128, O_MCOL + _s * 8, 8),
                                      fa(0, 128, O_MCOL + _s * 8, 8),
                                      _p.t[:, 0:8]),
                         waits=[prev_chunk_tn[gi], ve_pr])
        sc_modall = wv(VE)
        tn_modall = wv(TN)
        dma("gpsimd", GW0, wbuf_view(0),
            inw_h[:, 0:1024].rearrange("(a p) c -> p a c", p=128))
        inw0_tok = wv(GW0)
        dma("gpsimd", GW1, wbuf_view(1),
            inw_h[:, 1024:2048].rearrange("(a p) c -> p a c", p=128))
        inw1_tok = wv(GW1)

        op("vector", lambda e: e.tensor_scalar(fa(0, 128, O_ACOL, 8),
                                               fa(0, 128, O_MCOL + 8, 8), 1.0,
                                               None, ALU.add),
           waits=[sc_modall], chain=True, incs=[(VE, 1)])
        op("vector", lambda e: e.tensor_scalar(fa(0, 128, O_BCOL, 8),
                                               fa(0, 128, O_MCOL + 32, 8), 1.0,
                                               None, ALU.add),
           chain=True, incs=[(VE, 1)])

        def emit_clamp(dsto, sec):
            op("vector", lambda e: e.memset(fa(0, 128, O_GMX, 8), 0.0),
               waits=[sc_modall], chain=True)
            op("vector", lambda e, _s=sec: e.tensor_tensor(
                fa(0, 128, O_GMX, 8), fa(0, 128, O_MCOL + _s * 8, 8),
                fa(0, 128, O_GMX, 8), ALU.is_ge), chain=True)
            op("vector", lambda e: e.tensor_scalar(
                fa(0, 128, O_GMX, 8), fa(0, 128, O_GMX, 8), 2.0, -1.0,
                ALU.mult, ALU.add), chain=True, incs=[(VE, 1)])
            op("scalar", lambda e, _s=sec: e.activation(
                fa(0, 128, O_CLAMP, 8), fa(0, 128, O_MCOL + _s * 8, 8), AF.Abs),
               waits=[wv(VE), sc_modall], chain=True, incs=[(SC, 1)])
            op("vector", lambda e: e.tensor_scalar(
                fa(0, 128, O_CLAMP, 8), fa(0, 128, O_CLAMP, 8), 1.0, -0.25,
                ALU.mult, ALU.add), waits=[wv(SC)], chain=True)
            op("vector", lambda e: e.tensor_relu(
                fa(0, 128, O_CLAMP, 8), fa(0, 128, O_CLAMP, 8)), chain=True)
            op("vector", lambda e: e.tensor_scalar(
                fa(0, 128, O_CLAMP, 8), fa(0, 128, O_CLAMP, 8), 1.0, 0.25,
                ALU.mult, ALU.add), chain=True)
            op("vector", lambda e, _d=dsto: e.tensor_mul(
                fa(0, 128, _d, 8), fa(0, 128, O_GMX, 8),
                fa(0, 128, O_CLAMP, 8)), chain=True, incs=[(VE, 1)])

        emit_clamp(O_GACOL, 2)
        emit_clamp(O_GBCOL, 5)
        ve_cols = wv(VE)

        rows_s1 = lambda c0, w: fa(0, 1, O_ROWS + c0, w)
        rows_s2 = lambda c0, w: fa(0, 1, O_ROWS + TH + c0, w)
        trow = lambda c0, w: fa(0, 1, O_TSCR + c0, w)
        RBs = lambda c0, w: ba(0, 128, B_RB + c0, w)
        MRBs = lambda c0, w: ba(0, 128, B_MRB + c0, w)

        def emit_stats(use_slices, in_ready):
            tn_prev = None
            for si, (sc0, wd) in enumerate(use_slices):
                sq_waits = [in_ready] if si == 0 else [tn_prev]
                for kc in range(NCH):
                    op("scalar", lambda e, _kc=kc, _c0=sc0, _wd=wd:
                       e.activation(ba(0, 128, B_XSQ + _kc * 512, _wd),
                                    act_ch(_kc, _c0, _wd), AF.Square),
                       waits=(sq_waits if kc == 0 else ()),
                       incs=([(SC, 1)] if kc == NCH - 1 else []))
                sc_sq = wv(SC)
                wts1 = [PD.free] + ([in_ready] if si == 0 else [])
                for kc in range(NCH):
                    mm(PD, PD.t[0:1, 0:wd], onecol, act_ch(kc, sc0, wd),
                       kc == 0, kc == NCH - 1, waits=(wts1 if kc == 0 else ()))
                for kc in range(NCH):
                    mm(PD, PD.t[0:1, 512:512 + wd], onecol,
                       ba(0, 128, B_XSQ + kc * 512, wd),
                       kc == 0, kc == NCH - 1,
                       waits=([sc_sq] if kc == 0 else ()), inc=(kc == NCH - 1))
                tn_prev = wv(TN)
                evac("vector", PD, lambda e, _c0=sc0, _wd=wd:
                     e.tensor_copy(rows_s1(_c0, _wd), PD.t[0:1, 0:_wd]),
                     waits=[tn_prev], last=False)
                evac("vector", PD, lambda e, _c0=sc0, _wd=wd:
                     e.tensor_copy(rows_s2(_c0, _wd), PD.t[0:1, 512:512 + _wd]))
            W0 = use_slices[0][0]
            W1 = use_slices[-1][0] + use_slices[-1][1]
            w = W1 - W0
            op("vector", lambda e: e.tensor_scalar(rows_s1(W0, w), rows_s1(W0, w),
                                                   1.0 / DM, None, ALU.mult),
               chain=True)
            op("vector", lambda e: e.tensor_mul(trow(W0, w), rows_s1(W0, w),
                                                rows_s1(W0, w)), chain=True)
            op("vector", lambda e: e.tensor_scalar(rows_s2(W0, w), rows_s2(W0, w),
                                                   1.0 / DM, LN_EPS, ALU.mult,
                                                   ALU.add), chain=True)
            op("vector", lambda e: e.tensor_sub(rows_s2(W0, w), rows_s2(W0, w),
                                                trow(W0, w)),
               chain=True, incs=[(VE, 1)])
            ve_var = wv(VE)
            op("scalar", lambda e: e.activation(rows_s2(W0, w), rows_s2(W0, w),
                                                AF.Sqrt),
               waits=[ve_var], incs=[(SC, 1)])
            sc_sqrt = wv(SC)
            op("vector", lambda e: e.reciprocal(rows_s2(W0, w), rows_s2(W0, w)),
               waits=[sc_sqrt], chain=True)
            op("vector", lambda e: e.tensor_mul(rows_s1(W0, w), rows_s1(W0, w),
                                                rows_s2(W0, w)),
               chain=True, incs=[(VE, 1)])
            ve_rows = wv(VE)
            for bi, (dsto, ro, p) in enumerate(
                    [(B_RB, O_ROWS + TH, PB), (B_MRB, O_ROWS, PC)]):
                pend = None
                for si, (sc0, wd) in enumerate(use_slices):
                    po = (sc0 - W0) % 1024
                    fw = (pend is None)
                    if pend is None:
                        pend = sc0
                    op("tensor", lambda e, _c0=sc0, _wd=wd, _po=po, _p=p, _ro=ro:
                       e.matmul(_p.t[:, _po:_po + _wd], onerow_f,
                                fa(0, 1, _ro + _c0, _wd), start=True, stop=True),
                       waits=([_w for _w in (p.free, ve_rows) if _w] if fw else ()),
                       incs=([(TN, 1)] if po + wd >= 1024 or si == len(use_slices) - 1 else []))
                    if po + wd >= 1024 or si == len(use_slices) - 1:
                        tn_b = wv(TN)
                        lo, hi, plo = pend, sc0 + wd, (pend - W0) % 1024
                        evac("vector", p, lambda e, _lo=lo, _hi=hi, _plo=plo,
                             _d=dsto, _p=p:
                             e.tensor_copy(ba(0, 128, _d + _lo, _hi - _lo),
                                           _p.t[:, _plo:_plo + _hi - _lo]),
                             waits=[tn_b])
                        pend = None
            return wv(VE)

        def emit_affine(dst_is_xbuf, use_cols, scol, bcol_ap_fn, out_bf=True,
                        first_wait=()):
            off, w = use_cols
            toks = []
            for kc in range(NCH):
                op("vector", lambda e, _kc=kc: e.tensor_mul(
                    fa(0, 128, O_TSCR, w), act_ch(_kc, off, w), RBs(off, w)),
                   waits=(list(first_wait) if kc == 0 else ()))
                op("vector", lambda e: e.tensor_sub(
                    fa(0, 128, O_TSCR, w), fa(0, 128, O_TSCR, w), MRBs(off, w)))
                sa, ba_ = scol(kc), bcol_ap_fn(kc)
                if dst_is_xbuf:
                    dst = xbuf_ch(kc, off, w)
                else:
                    dst = act_ch(kc, off, w)
                op("vector", lambda e, _d=dst, _s=sa, _b=ba_: e.tensor_scalar(
                    _d, fa(0, 128, O_TSCR, w), _s, _b, ALU.mult, ALU.add),
                   incs=[(VE, 1)])
                toks.append(wv(VE))
            return wv(VE)

        ve_st_a = emit_stats(THS, g1_all)
        emit_affine(True, (0, TH),
                    lambda kc: fa(0, 128, O_ACOL + kc, 1),
                    lambda kc: fa(0, 128, O_MCOL + kc, 1),
                    first_wait=[ve_st_a, ve_cols, tn_modall])
        ve_h = wv(VE)
        ckpt("h", ve_h)

        for kc in range(NCH):
            op("vector", lambda e, _kc=kc: e.tensor_copy(
                act_ch(_kc, 0, TH), xbuf_ch(_kc, 0, TH)),
               incs=([(VE, 1)] if kc == NCH - 1 else []))
        ve_hb16 = wv(VE)
        ve_st_b = emit_stats(THS, ve_hb16)
        emit_affine(False, (0, TH),
                    lambda kc: vcol("ln1_w", kc), lambda kc: vcol("ln1_b", kc),
                    first_wait=[ve_st_b])
        for kc in range(NCH):
            op("vector", lambda e, _kc=kc: e.tensor_scalar(
                act_ch(_kc, 0, 3), act_ch(_kc, 0, 3), fa(0, 128, O_HM, 1),
                None, ALU.mult),
               waits=([sm_all] if kc == 0 else ()),
               incs=([(VE, 1)] if kc == NCH - 1 else []))
        ve_xn = wv(VE)

        for kc in range(NCH):
            for t0 in range(0, T, 512):
                wd = min(512, T - t0)
                op("vector", lambda e, _kc=kc, _t0=t0, _wd=wd: e.tensor_scalar(
                    fa(0, 128, O_TSCR, _wd), act_ch(_kc, _t0, _wd),
                    fa(0, 128, O_CW + _kc, 1), None, ALU.mult),
                   waits=([ve_xn, sm_all] if (kc == 0 and t0 == 0) else ()))
                for j in (1, 2, 3):
                    op("vector", lambda e, _kc=kc, _t0=t0, _wd=wd, _j=j:
                       e.scalar_tensor_tensor(
                           fa(0, 128, O_TSCR, _wd), act_ch(_kc, _t0 + _j, _wd),
                           fa(0, 128, O_CW + _j * 8 + _kc, 1),
                           fa(0, 128, O_TSCR, _wd), ALU.mult, ALU.add))
                op("vector", lambda e, _kc=kc, _t0=t0, _wd=wd:
                   e.scalar_tensor_tensor(
                       xbuf_ch(_kc, 3 + _t0, _wd), fa(0, 128, O_TSCR, _wd),
                       vcol("conv_b", _kc), xbuf_ch(_kc, 3 + _t0, _wd),
                       ALU.add, ALU.add),
                   incs=[(VE, 1)])
        ve_x2 = wv(VE)
        ckpt("x2", ve_x2)

        for kc in range(NCH):
            op("vector", lambda e, _kc=kc: e.tensor_copy(
                act_ch(_kc, 0, TH), xbuf_ch(_kc, 0, TH)),
               incs=([(VE, 1)] if kc == NCH - 1 else []))
        ve_x2b = wv(VE)
        ve_st_s = emit_stats(TS, ve_x2b)
        emit_affine(False, (3, T),
                    lambda kc: vcol("lns_w", kc), lambda kc: vcol("lns_b", kc),
                    first_wait=[ve_st_s])
        ve_xn2 = wv(VE)
        ckpt("lns", ve_xn2)

        tn_u_prev = None
        sz_writes = 0
        for ti in range(NT):
            t0 = 3 + ti * TT
            wd = min(TT, T - ti * TT)
            zslot = B_ZW0 if ti % 2 == 0 else B_ZW1
            for oc in range(16):
                p = PA if oc % 2 == 0 else PB
                wts = [p.free]
                if ti == 0 and oc == 0:
                    wts += [ve_xn2, inw0_tok]
                if ti == 0 and oc == 8:
                    wts += [inw1_tok]
                half = 0 if oc < 8 else 1
                occ = oc % 8
                for kc in range(NCH):
                    mm(p, p.t[:, 0:wd],
                       ba(0, 128, WBo[half] + kc * 1024 + occ * 128, 128),
                       act_ch(kc, t0, wd), kc == 0, kc == NCH - 1,
                       waits=(wts if kc == 0 else ()), inc=(kc == NCH - 1))
                tn_ip = wv(TN)
                if oc < 8:
                    ewts = [tn_ip] + ([tn_u_prev] if (oc == 0 and tn_u_prev) else [])
                    evac("scalar", p, lambda e, _oc=oc, _wd=wd, _p=p:
                         e.activation(ba(0, 128, B_XPT + _oc * 512, _wd),
                                      _p.t[:, 0:_wd], AF.Silu,
                                      bias=vcol("in_b", _oc)),
                         waits=ewts)
                else:
                    ewts = [tn_ip]
                    if oc == 8 and ti >= 2:
                        zsem = SZE if ti % 2 == 0 else SZO
                        ewts.append((zsem, zsem.n))
                    evac("scalar", p, lambda e, _oc=oc, _wd=wd, _p=p, _zs=zslot:
                         e.activation(ba(0, 128, _zs + (_oc - 8) * TT, _wd),
                                      _p.t[:, 0:_wd], AF.Sigmoid,
                                      bias=vcol("in_b", _oc)),
                         waits=ewts)
            sc_z = wv(SC)
            dma("sync", SZE if ti % 2 == 0 else SZO,
                zd.ap().rearrange("p (a t) -> p a t", a=8)[:, :, ti * TT:ti * TT + wd],
                ba(0, 128, zslot, ZSLAB).rearrange("p (a t) -> p a t", a=8)[:, :, 0:wd],
                waits=[sc_z])
            sz_writes += 1
            sc_xp = wv(SC)
            wts = [PC.free, sc_xp] + ([gf_small] if ti == 0 else [])
            for kc in range(NCH):
                mm(PC, PC.t[0:NS, 0:wd],
                   ba(0, 128, B_XPW + kc * 16, 16),
                   ba(0, 128, B_XPT + kc * 512, wd),
                   kc == 0, kc == NCH - 1,
                   waits=(wts if kc == 0 else ()),
                   inc=(kc == NCH - 1))
            tn_u_prev = wv(TN)
            evac("scalar", PC, lambda e, _ti=ti, _wd=wd: e.add(
                ba(0, NS, B_UFM + _ti * TT, _wd), PC.t[0:NS, 0:_wd],
                fa(0, NS, O_VEC + VP["xp_b"], 1)),
                 waits=[tn_u_prev])
        sc_u = wv(SC)
        sz_wr_all = [wv(SZE), wv(SZO)]

        op("scalar", lambda e: e.activation(fa(0, 1, O_LROW, NS),
                                            fa(0, 1, O_ALR, NS), AF.Exp),
           waits=[sm_all], incs=[(SC, 1)])
        op("vector", lambda e: e.tensor_scalar(fa(0, 1, O_LROW, NS),
                                               fa(0, 1, O_LROW, NS), -1.0,
                                               None, ALU.mult),
           waits=[wv(SC)], chain=True, incs=[(VE, 1)])
        ve_lrow = wv(VE)
        op("tensor", lambda e: e.matmul(PD.t[:, 0:NS], onerow_f,
                                        fa(0, 1, O_LROW, NS),
                                        start=True, stop=True),
           waits=[PD.free, ve_lrow], incs=[(TN, 1)])
        evac("vector", PD, lambda e: e.tensor_copy(fa(0, 128, O_LAMN, NS),
                                                   PD.t[:, 0:NS]),
             waits=[wv(TN)])
        ve_lamn = wv(VE)

        op("scalar", lambda e: e.activation(fa(0, NS, O_LAMC, 1),
                                            fa(0, NS, O_VEC + VP["A_log"], 1),
                                            AF.Exp),
           waits=[sm_all], incs=[(SC, 1)])
        sc_lamc = wv(SC)
        op("vector", lambda e: e.tensor_scalar(fa(0, NS, O_A128, 1),
                                               fa(0, NS, O_LAMC, 1), -128.0,
                                               None, ALU.mult),
           waits=[sc_lamc], chain=True, incs=[(VE, 1)])
        op("scalar", lambda e: e.activation(fa(0, NS, O_A128, 1),
                                            fa(0, NS, O_A128, 1), AF.Exp),
           waits=[wv(VE)], chain=True, incs=[(SC, 1)])
        sc_a128 = wv(SC)

        tn_now = wv(TN)
        op("gpsimd", lambda e: e.iota(fa(0, NS, O_PPOW, T), pattern=[[1, T]],
                                      base=1, channel_multiplier=0,
                                      allow_small_or_imprecise_dtypes=True),
           waits=[tn_now], incs=[(GP, 1)])
        gp_i1 = wv(GP)
        op("vector", lambda e: e.tensor_scalar(fa(0, NS, O_PPOW, T),
                                               fa(0, NS, O_PPOW, T),
                                               fa(0, NS, O_LAMC, 1), None,
                                               ALU.mult),
           waits=[gp_i1, sc_lamc, tn_now], chain=True)
        op("vector", lambda e: e.tensor_scalar(fa(0, NS, O_PPOW, T),
                                               fa(0, NS, O_PPOW, T), -1.0,
                                               None, ALU.mult),
           chain=True, incs=[(VE, 1)])
        op("scalar", lambda e: e.activation(fa(0, NS, O_PPOW, T),
                                            fa(0, NS, O_PPOW, T), AF.Exp),
           waits=[wv(VE)], chain=True, incs=[(SC, 1)])
        sc_ppow = wv(SC)

        op("gpsimd", lambda e: e.iota(fa(0, 128, O_TSCR, 128),
                                      pattern=[[1, 128]], base=0,
                                      channel_multiplier=-1,
                                      allow_small_or_imprecise_dtypes=True),
           waits=[ve_xn2], incs=[(GP, 1)])
        gp_i2 = wv(GP)
        op("gpsimd", lambda e: e.affine_select(fa(0, 128, O_TSCR, 128),
                                               fa(0, 128, O_TSCR, 128),
                                               pattern=[[1, 128]],
                                               compare_op=ALU.is_ge, fill=1e9,
                                               base=0, channel_multiplier=-1),
           waits=[gp_i2], incs=[(GP, 1)])
        gp_dm = wv(GP)
        sc_tt_prev = {0: None, 1: None}
        for n in range(NS):
            par = n % 2
            wts = [gp_dm, ve_lamn] if n == 0 else []
            if sc_tt_prev[par] is not None:
                wts.append(sc_tt_prev[par])
            op("vector", lambda e, _n=n, _par=par: e.tensor_scalar(
                fa(0, 128, O_TSCR + 128 + _par * 128, 128),
                fa(0, 128, O_TSCR, 128),
                fa(0, 128, O_LAMN + _n, 1), None, ALU.mult),
               waits=wts, incs=[(VE, 1)])
            ve_tt = wv(VE)
            op("scalar", lambda e, _n=n, _par=par: e.activation(
                ba(0, 128, B_TTALL + _n * 128, 128),
                fa(0, 128, O_TSCR + 128 + _par * 128, 128), AF.Exp),
               waits=[ve_tt], incs=[(SC, 1)])
            sc_tt_prev[par] = wv(SC)
        sc_tt = wv(SC)

        op("tensor", lambda e: e.matmul(PD.t[:, 16:17], sel16,
                                        fa(0, NS, O_LAMC, 1),
                                        start=True, stop=True),
           waits=[PD.free, sc_lamc], incs=[(TN, 1)])
        evac("vector", PD, lambda e: e.tensor_scalar(fa(0, 128, O_LAMT, 1),
                                                     PD.t[:, 16:17], -1.0,
                                                     None, ALU.mult),
             waits=[wv(TN), wv(VE)])
        ve_lamt = wv(VE)

        dma("gpsimd", GU, ccin.ap(), ba(0, NS, B_UFM, T), waits=[sc_u])
        gu_ccin = wv(GU)
        op("gpsimd", lambda e: e.collective_compute(
            "AllGather", ALU.bypass, replica_groups=[list(range(NCORES))],
            ins=[ccin.ap().opt()], outs=[ccout.ap().opt()]),
           waits=[wv(GU)], incs=[(CC, 1)])
        dma("gpsimd", GU, ba(0, 128, B_UALL, T), ccout.ap(), waits=[wv(CC)])
        gu_all = wv(GU)

        for c in range(SCH):
            p = PA if c % 2 == 0 else PB
            op("tensor", lambda e, _c=c, _p=p: e.matmul(
                _p.t[:, 0:NS], ba(0, NS, B_UFM + _c * 128, 128), id16b,
                start=True, stop=True),
               waits=[p.free] + ([sc_u] if c == 0 else []), incs=[(TN, 1)])
            tn_t = wv(TN)
            evac("vector", p, lambda e, _c=c, _p=p: e.tensor_copy(
                ba(0, 128, B_TALL + _c * 16, 16), _p.t[:, 0:NS]),
                 waits=[tn_t])
        ve_tall = wv(VE)
        tn_utrans = wv(TN)
        for n in range(NS):
            mm(PC, PC.t[:, n * SCH:(n + 1) * SCH],
               ba(0, 128, B_TTALL + n * 128, 128),
               ABF[0:128, B_TALL + n:B_TALL + 16 * SCH:16], True, True,
               waits=([PC.free, ve_tall, sc_tt] if n == 0 else ()),
               inc=(n == NS - 1))
        tn_scan = wv(TN)
        evac("vector", PC, lambda e: e.tensor_copy(
            fa(0, 128, O_HLOC, 16 * SCH), PC.t[:, 0:16 * SCH]),
             waits=[tn_scan])
        ve_hloc = wv(VE)
        for c in range(SCH):
            p = PA if c % 2 == 0 else PB
            op("tensor", lambda e, _c=c, _p=p: e.matmul(
                _p.t[0:NS, 0:128],
                AF32[0:128, O_HLOC + _c:O_HLOC + 16 * SCH:SCH],
                id128f, start=True, stop=True),
               waits=[p.free] + ([ve_hloc] if c == 0 else []),
               incs=[(TN, 1)])
            tn_hb = wv(TN)
            evac("vector", p, lambda e, _c=c, _p=p: e.tensor_copy(
                fa(0, NS, O_HFM + _c * 128, 128), _p.t[0:NS, 0:128]),
                 waits=[tn_hb])
        ve_hfm = wv(VE)

        dma("sync", SX, ba(0, 128, B_CEXP, T), ce_h, waits=[wv(TN)])
        sx_ce = wv(SX)
        op("vector", lambda e: e.tensor_scalar(fa(0, 128, O_TSCR, T),
                                               ba(0, 128, B_CEXP, T),
                                               fa(0, 128, O_LAMT, 1), None,
                                               ALU.mult),
           waits=[ve_lamt, sx_ce, sc_tt], incs=[(VE, 1)])
        op("scalar", lambda e: e.activation(fa(0, 128, O_TSCR, T),
                                            fa(0, 128, O_TSCR, T), AF.Exp),
           waits=[wv(VE)], chain=True, incs=[(SC, 1)])
        op("vector", lambda e: e.tensor_mul(fa(0, 128, O_TSCR, T),
                                            fa(0, 128, O_TSCR, T),
                                            ba(0, 128, B_UALL, T)),
           waits=[wv(SC), gu_all], chain=True)
        op("vector", lambda e: e.reduce_sum(fa(0, 128, O_RSUM, 1),
                                            fa(0, 128, O_TSCR, T), AX.X),
           chain=True, incs=[(VE, 1)])
        ve_rsum = wv(VE)
        op("tensor", lambda e: e.matmul(PD.t[0:NS, 20:21], selt,
                                        fa(0, 128, O_RSUM, 1),
                                        start=True, stop=True),
           waits=[PD.free, ve_rsum], incs=[(TN, 1)])
        evac("vector", PD, lambda e: e.tensor_copy(fa(0, NS, O_CARRY, 1),
                                                   PD.t[0:NS, 20:21]),
             waits=[wv(TN)])
        op("vector", lambda e: e.tensor_copy(fa(0, NS, O_EVEC, 1),
                                             fa(0, NS, O_CARRY, 1)),
           chain=True)
        for c in range(1, SCH + 1):
            op("vector", lambda e, _c=c: e.scalar_tensor_tensor(
                fa(0, NS, O_EVEC + _c, 1), fa(0, NS, O_EVEC + _c - 1, 1),
                fa(0, NS, O_A128, 1),
                fa(0, NS, O_HFM + (_c - 1) * 128 + 127, 1),
                ALU.mult, ALU.add),
               waits=([sc_a128, ve_hfm] if c == 1 else ()), chain=True)
        for c in range(SCH):
            op("vector", lambda e, _c=c: e.scalar_tensor_tensor(
                fa(0, NS, O_HFM + _c * 128, 128), fa(0, NS, O_PPOW, 128),
                fa(0, NS, O_EVEC + _c, 1), fa(0, NS, O_HFM + _c * 128, 128),
                ALU.mult, ALU.add),
               waits=([sc_ppow] if c == 0 else ()), chain=True)
        op("vector", lambda e: e.tensor_copy(ba(0, NS, B_HBF, T),
                                             fa(0, NS, O_HFM, T)),
           chain=True, incs=[(VE, 1)])
        ve_hbf = wv(VE)
        if stop_after == "scan":
            dma("gpsimd", GU, out_h[0:128, 0:T], ba(0, 128, B_UALL, T),
                waits=[ve_hbf])
            dma("sync", SO, out_h[128:144, 0:T], fa(0, NS, O_HFM, T),
                waits=[ve_hbf])
            dma("sync", SO, out_h[144:160, 0:SCH + 1],
                fa(0, NS, O_EVEC, SCH + 1), slow=True)
            dma("sync", SO, out_h[160:176, 0:1], fa(0, NS, O_CARRY, 1), slow=True)
            dma("sync", SO, out_h[176:304, 0:1], fa(0, 128, O_RSUM, 1), slow=True)
            dma("gpsimd", GU, out_h[304:320, 0:T], ba(0, NS, B_UFM, T))
            dma("sync", SO, out_h[320:336, 0:T], fa(0, NS, O_PPOW, T))
            dma("sync", SO, out_h[336:464, 0:T], fa(0, 128, O_TSCR, T))
            op("gpsimd", lambda e: e.wait_ge(SO.h, SO.n))
            op("gpsimd", lambda e: e.wait_ge(GU.h, GU.n))
            _bstate["stopped"] = True

        dma("gpsimd", GF, ba(0, 128, B_OUTW, 8192).rearrange("p (a c) -> p a c", a=8),
            ow_h.rearrange("(a p) c -> p a c", p=128),
            waits=sz_wr_all)
        gf_outw = wv(GF)
        zrd_tok = None
        for ti in range(NT):
            t0 = ti * TT
            wd = min(TT, T - t0)
            zw = list(sz_wr_all) + ([zrd_tok] if zrd_tok else
                                    [tn_utrans, gu_ccin])
            dma("sync", SZR,
                ba(0, 128, B_ZRD, ZSLAB).rearrange("p (a t) -> p a t", a=8)[:, :, 0:wd],
                zd.ap().rearrange("p (a t) -> p a t", a=8)[:, :, t0:t0 + wd],
                waits=zw)
            sz_rd = wv(SZR)
            for oc in range(8):
                p = PA if oc % 2 == 0 else PB
                wts = [p.free] + ([ve_hbf, gf_small] if (ti == 0 and oc == 0) else [])
                mm(p, p.t[:, 0:wd], ba(0, NS, B_DREG + oc * 128, 128),
                   ba(0, NS, B_HBF + t0, wd), True, True, waits=wts, inc=True)
                tn_d = wv(TN)
                evac("vector", p, lambda e, _oc=oc, _wd=wd, _p=p:
                     e.tensor_mul(ba(0, 128, B_YSZ + _oc * 512, _wd),
                                  _p.t[:, 0:_wd],
                                  ba(0, 128, B_ZRD + _oc * TT, _wd)),
                     waits=[tn_d, sz_rd])
            ve_ysz = wv(VE)
            zrd_tok = ve_ysz
            for oc in range(8):
                p = PC if oc % 2 == 0 else PD
                wts = [p.free, ve_ysz] + ([gf_outw] if (ti == 0 and oc == 0) else [])
                for kc in range(NCH):
                    mm(p, p.t[:, 0:wd],
                       ba(0, 128, B_OUTW + kc * 1024 + oc * 128, 128),
                       ba(0, 128, B_YSZ + kc * 512, wd),
                       kc == 0, kc == NCH - 1,
                       waits=(wts if kc == 0 else ()), inc=(kc == NCH - 1))
                tn_ow = wv(TN)
                evac("vector", p, lambda e, _oc=oc, _t0=t0, _wd=wd, _p=p:
                     e.scalar_tensor_tensor(
                         xbuf_ch(_oc, 3 + _t0, _wd), _p.t[:, 0:_wd],
                         vcol("out_b", _oc), xbuf_ch(_oc, 3 + _t0, _wd),
                         ALU.add, ALU.add),
                     waits=[tn_ow])
        ve_x3 = wv(VE)
        ckpt("x3", ve_x3)

        for kc in range(NCH):
            op("vector", lambda e, _kc=kc: e.tensor_copy(
                act_ch(_kc, 0, TH), xbuf_ch(_kc, 0, TH)),
               waits=([ve_x3] if kc == 0 else ()),
               incs=([(VE, 1)] if kc == NCH - 1 else []))
        ve_x3b = wv(VE)
        ve_st_2 = emit_stats(TS, ve_x3b)
        emit_affine(False, (3, T),
                    lambda kc: vcol("ln2_w", kc), lambda kc: vcol("ln2_b", kc),
                    first_wait=[ve_st_2])
        ve_h2 = wv(VE)

        wslot = {"i": 0}
        wb_last2 = [wv(TN), wv(TN)]

        def load_slice(src, extra_waits=()):
            i = wslot["i"] % 2
            wslot["i"] += 1
            sem = GWs[i]
            dma("gpsimd", sem, wbuf_view(i), src,
                waits=[wb_last2[i]] + list(extra_waits))
            return i, (sem, sem.n)

        def w1_slice(h, g):
            return h[:, g * 1024:(g + 1) * 1024].rearrange("(a p) c -> p a c", p=128)

        def w2_slice(h, g):
            return h[g * 1024:(g + 1) * 1024, :].rearrange("(a p) c -> p a c", p=128)

        def ffn_group(i1, tok1, i2, tok2, in_off_fn, masked, extra_mm,
                      bias_base, first_waits=()):
            gidx = _bstate.get("ffn_gidx", 0)
            _bstate["ffn_gidx"] = gidx + 1
            tpar = {0: None, 1: None}
            for ffc in range(8):
                p = PA if ffc % 2 == 0 else PB
                wts = [p.free, tok1] + (list(first_waits) if ffc == 0 else [])
                for s in range(NHB):
                    for kc in range(NCH):
                        mm(p, p.t[:, s * STP:s * STP + STP],
                           ba(0, 128, WBo[i1] + kc * 1024 + ffc * 128, 128),
                           in_off_fn(kc, s),
                           kc == 0, kc == NCH - 1,
                           waits=(wts if (kc == 0 and s == 0) else ()),
                           inc=(kc == NCH - 1 and s == NHB - 1))
                tn_a = wv(TN)
                if masked is None:
                    evac("scalar", p, lambda e, _f=ffc, _p=p:
                         e.activation(ba(0, 128, B_G1 + _f * Th, Th),
                                      _p.t[:, 0:Th], AF.Gelu,
                                      bias=fa(0, 128, O_VEC + bias_base + _f, 1)),
                         waits=[tn_a])
                else:
                    par = ffc % 2
                    toff = O_TSCR + par * Th
                    wts2 = [tn_a] + ([tpar[par]] if tpar[par] is not None else [])
                    evac("vector", p, lambda e, _f=ffc, _p=p, _to=toff:
                         e.scalar_tensor_tensor(
                             fa(0, 128, _to, Th), masked,
                             fa(0, 128, O_VEC + bias_base + _f, 1),
                             _p.t[:, 0:Th], ALU.mult, ALU.add),
                         waits=wts2)
                    op("scalar", lambda e, _f=ffc, _to=toff:
                       e.activation(ba(0, 128, B_G1 + _f * Th, Th),
                                    fa(0, 128, _to, Th), AF.Gelu),
                       waits=[wv(VE)], incs=[(SC, 1)])
                    tpar[par] = wv(SC)
                if gidx == 0 and stop_after == f"ffc{ffc}":
                    ckpt(f"ffc{ffc}", wv(SC))
                    return
            sc_a = wv(SC)
            _bstate["tn_a_end"] = wv(TN)
            if gidx == 0 and stop_after == "ffnA":
                ckpt("ffnA", sc_a)
                return
            for oc in range(8):
                p = PC if oc % 2 == 0 else PD
                wts = [p.free, sc_a, tok2]
                has_extra = bool(extra_mm)
                for s in range(NHB):
                    for ffc in range(8):
                        last = (ffc == 7 and s == NHB - 1)
                        mm(p, p.t[:, s * STP:s * STP + STP],
                           ba(0, 128, WBo[i2] + ffc * 1024 + oc * 128, 128),
                           ba(0, 128, B_G1 + ffc * Th + s * STP, STP),
                           ffc == 0, ffc == 7 and not has_extra,
                           waits=(wts if (ffc == 0 and s == 0) else ()),
                           inc=(last and not has_extra))
                for em in extra_mm:
                    em(oc, p)
                if gidx == 0 and stop_after == f"ffnB{oc}":
                    ckpt(f"ffnB{oc}", wv(TN))
                    return
                yield oc, p, wv(TN)

        first = [ve_h2]
        for hh in range(2):
            for g in range(2):
                i1, tok1 = load_slice(w1_slice(m1_h, g))
                i2, tok2 = load_slice(w2_slice(m2_h, g))
                hof = 3 + hh * Th

                def mkin(_hof):
                    return lambda kc, s: act_ch(kc, _hof + s * STP, STP)
                gen = ffn_group(i1, tok1, i2, tok2, mkin(hof), None, [],
                                VP["mlp_b1"] + g * 8, first_waits=first)
                first = []
                for oc, p, tn_b in gen:
                    if g == 1:
                        evac("vector", p, lambda e, _oc=oc, _h=hof, _p=p:
                             e.scalar_tensor_tensor(
                                 xbuf_ch(_oc, _h, Th), _p.t[:, 0:Th],
                                 vcol("mlp_b2", _oc), xbuf_ch(_oc, _h, Th),
                                 ALU.add, ALU.add),
                             waits=[tn_b])
                    else:
                        evac("vector", p, lambda e, _oc=oc, _h=hof, _p=p:
                             e.scalar_tensor_tensor(
                                 xbuf_ch(_oc, _h, Th), _p.t[:, 0:Th], 0.0,
                                 xbuf_ch(_oc, _h, Th), ALU.add, ALU.add),
                             waits=[tn_b])
                    if stop_after == f"ffnE{oc}" and hh == 0 and g == 0:
                        ckpt(f"ffnE{oc}", wv(VE))
                        break
                wb_last2[i1] = _bstate.get("tn_a_end") or wv(TN)
                wb_last2[i2] = wv(TN)
                if stop_after == "mlp0" and hh == 0 and g == 0:
                    ckpt("mlp0", wv(VE))
        ve_mlp = wv(VE)
        ckpt("mlp", ve_mlp)

        comb = []
        nhalf = max(1, T // 1024)
        hw2 = min(1024, T)
        for kc in range(NCH):
            for hf in range(nhalf):
                idx = kc * nhalf + hf
                slot = O_TSCR + (idx % 2) * hw2
                psem = SXA if idx % 2 == 0 else SXB
                wts = ([comb[idx - 2]] if idx >= 2 else [ve_mlp])
                dma("sync", psem, fa(0, 128, slot, hw2),
                    x_h[kc * 128:(kc + 1) * 128,
                        3 + hf * hw2:3 + hf * hw2 + hw2],
                    waits=wts)
                sx_c = wv(psem)
                op("vector", lambda e, _kc=kc, _hf=hf, _s=slot:
                   e.scalar_tensor_tensor(
                       xbuf_ch(_kc, 3 + _hf * hw2, hw2),
                       xbuf_ch(_kc, 3 + _hf * hw2, hw2),
                       fa(0, 128, O_GACOL + _kc, 1), fa(0, 128, _s, hw2),
                       ALU.mult, ALU.add),
                   waits=[sx_c, ve_cols] + ([ve_mlp] if idx == 0 else []),
                   incs=[(VE, 1)])
                comb.append(wv(VE))
        ve_xnew = wv(VE)
        ckpt("xnew", ve_xnew)

        for kc in range(NCH):
            op("vector", lambda e, _kc=kc: e.tensor_copy(
                act_ch(_kc, 0, TH), xbuf_ch(_kc, 0, TH)),
               waits=([ve_xnew] if kc == 0 else ()),
               incs=([(VE, 1)] if kc == NCH - 1 else []))
        ve_xnb = wv(VE)
        ve_st_bb = emit_stats(TS, ve_xnb)
        emit_affine(False, (3, T),
                    lambda kc: fa(0, 128, O_BCOL + kc, 1),
                    lambda kc: fa(0, 128, O_MCOL + 24 + kc, 1),
                    first_wait=[ve_st_bb])
        ve_hb = wv(VE)

        for ti in range(NT):
            t0 = 3 + ti * TT
            wd = min(TT, T - ti * TT)
            wts = [PA.free] + ([ve_hb, gf_small] if ti == 0 else [])
            for kc in range(NCH):
                mm(PA, PA.t[0:4, 0:wd], ba(0, 128, B_WGB + kc * 4, 4),
                   act_ch(kc, t0, wd), kc == 0, kc == NCH - 1,
                   waits=(wts if kc == 0 else ()), inc=(kc == NCH - 1))
            tn_g = wv(TN)
            evac("scalar", PA, lambda e, _ti=ti, _wd=wd: e.add(
                fa(0, 4, O_GATE + _ti * TT, _wd), PA.t[0:4, 0:_wd],
                fa(0, 4, O_VEC + VP["bg"], 1)),
                 waits=[tn_g])
        sc_gate = wv(SC)
        for c in range(NTC):
            op("tensor", lambda e, _c=c: e.matmul(
                PB.t[0:128, 0:4], fa(0, 4, O_GATE + _c * 128, 128), id4f,
                start=True, stop=True),
               waits=[PB.free] + ([sc_gate] if c == 0 else []),
               incs=[(TN, 1)])
            tn_gt = wv(TN)
            op("vector", lambda e: e.tensor_copy(fa(0, 128, O_GTM, 4),
                                                 PB.t[0:128, 0:4]),
               waits=[tn_gt], chain=True, incs=[(VE, 1)])
            PB.free = wv(VE)
            op("vector", lambda e: e.reduce_max(fa(0, 128, O_GMX, 1),
                                                fa(0, 128, O_GTM, 4), AX.X),
               chain=True)
            for ei in range(4):
                op("vector", lambda e, _e=ei: e.tensor_tensor(
                    fa(0, 128, O_GMX + 1 + _e, 1), fa(0, 128, O_GTM + _e, 1),
                    fa(0, 128, O_GMX, 1), ALU.is_ge), chain=True)
            op("vector", lambda e, _c=c: e.tensor_copy(
                fa(0, 128, O_SELTM + _c * 4, 1), fa(0, 128, O_GMX + 1, 1)),
               chain=True)
            op("vector", lambda e: e.tensor_scalar(
                fa(0, 128, O_GMX + 5, 1), fa(0, 128, O_GMX + 1, 1), -1.0, 1.0,
                ALU.mult, ALU.add), chain=True)
            op("vector", lambda e, _c=c: e.tensor_mul(
                fa(0, 128, O_SELTM + _c * 4 + 1, 1), fa(0, 128, O_GMX + 2, 1),
                fa(0, 128, O_GMX + 5, 1)), chain=True)
            op("vector", lambda e, _c=c: e.tensor_sub(
                fa(0, 128, O_GMX + 6, 1), fa(0, 128, O_GMX + 5, 1),
                fa(0, 128, O_SELTM + _c * 4 + 1, 1)), chain=True)
            op("vector", lambda e, _c=c: e.tensor_mul(
                fa(0, 128, O_SELTM + _c * 4 + 2, 1), fa(0, 128, O_GMX + 3, 1),
                fa(0, 128, O_GMX + 6, 1)), chain=True)
            op("vector", lambda e, _c=c: e.tensor_sub(
                fa(0, 128, O_SELTM + _c * 4 + 3, 1), fa(0, 128, O_GMX + 6, 1),
                fa(0, 128, O_SELTM + _c * 4 + 2, 1)),
               chain=True, incs=[(VE, 1)])
        ve_sel = wv(VE)
        for c in range(NTC):
            op("tensor", lambda e, _c=c: e.matmul(
                PC.t[0:4, 0:128], fa(0, 128, O_SELTM + _c * 4, 4), id128f,
                start=True, stop=True),
               waits=[PC.free] + ([ve_sel] if c == 0 else []),
               incs=[(TN, 1)])
            tn_m = wv(TN)
            evac("vector", PC, lambda e, _c=c: e.tensor_copy(
                ba(0, 4, B_MASK4 + _c * 128, 128), PC.t[0:4, 0:128]),
                 waits=[tn_m])
        ve_mask4 = wv(VE)
        mask4 = lambda off, w: ba(0, 4, B_MASK4 + off, w)
        if stop_after == "masks":
            dma("gpsimd", GU, out_h[0:4, 0:T], ba(0, 4, B_MASK4, T),
                waits=[ve_mask4])
            dma("sync", SO, out_h[8:12, 0:T], fa(0, 4, O_GATE, T),
                waits=[ve_mask4])
            dma("sync", SO, out_h[16:144, 0:4 * NTC], fa(0, 128, O_SELTM, 4 * NTC),
                slow=True)
            dma("sync", SO, out_h[144:272, 0:T], fa(0, 128, O_TSCR, T))
            dma("sync", SO, out_h[272:400, 0:48], fa(0, 128, O_MCOL, 48),
                slow=True)
            dma("sync", SO, out_h[400:528, 0:40], fa(0, 128, O_ACOL, 40),
                slow=True)
            op("gpsimd", lambda e: e.wait_ge(SO.h, SO.n))
            op("gpsimd", lambda e: e.wait_ge(GU.h, GU.n))
            _bstate["stopped"] = True

        dma("gpsimd", GF, ba(0, NE, B_DREG, DM), b2m_h, waits=[wv(TN)])
        gf_b2m = wv(GF)

        def m2_mm(hh):
            def add(oc, p):
                for s in range(NHB):
                    mm(p, p.t[:, s * STP:s * STP + STP],
                       ba(0, 4, B_DREG + oc * 128, 128),
                       mask4(hh * Th + s * STP, STP),
                       False, True,
                       waits=([gf_b2m] if (oc == 0 and s == 0) else ()),
                       inc=(s == NHB - 1))
            return add

        for hh in range(2):
            for ei in range(NE):
                wts = [PA.free, ve_mask4]
                for s in range(NHB):
                    mm(PA, PA.t[:, s * STP:s * STP + STP],
                       ba(0, 4, B_CBF + CB_SELROW + ei * 128, 128),
                       mask4(hh * Th + s * STP, STP), True, True,
                       waits=(wts if s == 0 else ()), inc=(s == NHB - 1))
                tn_mbc = wv(TN)
                evac("vector", PA, lambda e: e.tensor_copy(
                    ba(0, 128, B_MBC, Th), PA.t[:, 0:Th]),
                     waits=[tn_mbc])
                ve_mbc = wv(VE)
                for kc in range(NCH):
                    op("vector", lambda e, _kc=kc, _hh=hh: e.tensor_mul(
                        ba(0, 128, B_HMI + _kc * Th, Th),
                        act_ch(_kc, 3 + _hh * Th, Th),
                        ba(0, 128, B_MBC, Th)),
                       waits=([ve_mbc] if kc == 0 else ()),
                       incs=([(VE, 1)] if kc == NCH - 1 else []))
                ve_hmi = wv(VE)
                for g in range(4):
                    i1, tok1 = load_slice(w1_slice(f1_h[ei], g))
                    i2, tok2 = load_slice(w2_slice(f2_h[ei], g))
                    extra = [m2_mm(hh)] if (ei == 0 and g == 0) else []

                    def mkin2():
                        return lambda kc, s: ba(0, 128, B_HMI + kc * Th + s * STP, STP)
                    gen = ffn_group(i1, tok1, i2, tok2, mkin2(),
                                    ba(0, 128, B_MBC, Th), extra,
                                    VP["b1"] + ei * 32 + g * 8,
                                    first_waits=([ve_hmi] if g == 0 else []))
                    for oc, p, tn_b in gen:
                        evac("vector", p, lambda e, _oc=oc, _hh=hh, _p=p:
                             e.scalar_tensor_tensor(
                                 xbuf_ch(_oc, 3 + _hh * Th, Th), _p.t[:, 0:Th],
                                 fa(0, 128, O_GBCOL + _oc, 1),
                                 xbuf_ch(_oc, 3 + _hh * Th, Th),
                                 ALU.mult, ALU.add),
                             waits=[tn_b])
                    wb_last2[i1] = _bstate.get("tn_a_end") or wv(TN)
                    wb_last2[i2] = wv(TN)
            ve_half = wv(VE)
            for oc in range(NCH):
                dma("sync", SO, out_h[oc * 128:(oc + 1) * 128,
                                      hh * Th:hh * Th + Th],
                    xbuf_ch(oc, 3 + hh * Th, Th),
                    waits=([ve_half] if oc == 0 else []))
        so_all = wv(SO)
        op("gpsimd", lambda e: e.wait_ge(SO.h, so_all[1]))

        with nc.Block() as block:
            @block.sync
            def _(e):
                for f in Q["sync"]:
                    f(e)

            @block.gpsimd
            def _(e):
                for f in Q["gpsimd"]:
                    f(e)

            @block.tensor
            def _(e):
                for f in Q["tensor"]:
                    f(e)

            @block.vector
            def _(e):
                for f in Q["vector"]:
                    f(e)

            @block.scalar
            def _(e):
                for f in Q["scalar"]:
                    f(e)

    return nc


LAYOUT = {}
_PROG_CACHE = {}


def get_program(T, stop_after=None):
    key = (T, stop_after)
    if key not in _PROG_CACHE:
        _PROG_CACHE[key] = build_program(T, stop_after)
    return _PROG_CACHE[key]


def kernel(**inputs):
    S = np.asarray(inputs["L0_attn_x"]).shape[1]
    T = S // NCORES
    in_maps = host_pack(inputs, T)
    nc = get_program(T)
    res = run_bass_kernel_spmd(nc, in_maps, core_ids=list(range(NCORES)))
    parts = [np.asarray(res.results[k]["out"]) for k in range(NCORES)]
    full = np.concatenate(parts, axis=1)
    return np.ascontiguousarray(full.T)[None, :, :].astype(np.float32)
